# version 11
# speedup vs baseline: 1.0338x; 1.0003x over previous
import numpy as np
import ml_dtypes
import concourse.bass as bass
import concourse.mybir as mybir
from concourse.bass_utils import run_bass_kernel_spmd

F32 = mybir.dt.float32
BF16 = mybir.dt.bfloat16
AF = mybir.ActivationFunctionType
ALU = mybir.AluOpType

NB = 2
T = 2048
D = 1024
DI = 2048
NST = 32
DTR = 64
H = 1024
C = 512
KD = D // 128
KE = DI // 128

BF = ml_dtypes.bfloat16
ACT_CONSTS = [1e-5, 1e-6, -0.5, -1.0, 1.0, 0.5, 2.0, 0.0] + [-float(j) for j in range(2, 14)]
DEBUG_A = False
DEBUG_B = False


class Prog:

    ENGINES = ("sync", "tensor", "scalar", "vector", "gpsimd")

    def __init__(self, nc, sems):
        self.nc = nc
        self.items = {e: [] for e in self.ENGINES}
        self.semh = dict(sems)
        self.cnt = {k: 0 for k in self.semh}
        self.last_wait = {}

    def raw(self, engine, fn):
        self.items[engine].append(fn)

    def inc(self, engine, sem_name, fn, k=1):
        h = self.semh[sem_name]
        self.cnt[sem_name] += k

        def wrapped(eng, fn=fn, h=h, k=k):
            fn(eng).then_inc(h, k)

        self.items[engine].append(wrapped)
        return self.cnt[sem_name]

    def dma(self, out, in_, engine="sync"):
        return self.inc(engine, "sd",
                        lambda eng, o=out, i=in_: eng.dma_start(out=o, in_=i), 16)

    def wait(self, engine, sem_name, count=None):
        tgt = self.cnt[sem_name] if count is None else count
        if tgt <= 0:
            return
        key = (engine, sem_name)
        if self.last_wait.get(key, -1) >= tgt:
            return
        self.last_wait[key] = tgt
        h = self.semh[sem_name]

        def wrapped(eng, h=h, tgt=tgt):
            eng.wait_ge(h, tgt)

        self.items[engine].append(wrapped)

    def barrier(self):
        for e in self.ENGINES:
            for s in self.semh:
                self.wait(e, s)

    def build_block(self):
        nc = self.nc
        with nc.Block() as block:
            @block.sync
            def _(eng):
                for fn in self.items["sync"]:
                    fn(eng)

            @block.tensor
            def _(eng):
                for fn in self.items["tensor"]:
                    fn(eng)

            @block.scalar
            def _(eng):
                for fn in self.items["scalar"]:
                    fn(eng)

            @block.vector
            def _(eng):
                for fn in self.items["vector"]:
                    fn(eng)

            @block.gpsimd
            def _(eng):
                for fn in self.items["gpsimd"]:
                    fn(eng)


def _col(buf, i):
    return buf[:, i:i + 1]



def build_phase_a():
    nc = bass.Bass("TRN2", target_bir_lowering=False, debug=False)
    xT = nc.dram_tensor("xT", [NB, D, T], F32, kind="ExternalInput").ap()
    winT = nc.dram_tensor("winT", [D, 2 * DI], F32, kind="ExternalInput").ap()
    xprojT = nc.dram_tensor("xprojT", [DI, 128], F32, kind="ExternalInput").ap()
    dtprojT = nc.dram_tensor("dtprojT", [DTR, DI], F32, kind="ExternalInput").ap()
    outprojT = nc.dram_tensor("outprojT", [DI, D], F32, kind="ExternalInput").ap()
    alog_t = nc.dram_tensor("alog_t", [128, KE * NST], F32, kind="ExternalInput").ap()
    cols = nc.dram_tensor("cols", [128, KE * 4], F32, kind="ExternalInput").ap()
    convw = nc.dram_tensor("convw", [128, KE * 4], F32, kind="ExternalInput").ap()
    consts_b = nc.dram_tensor("consts_b", [128, 130], BF16, kind="ExternalInput").ap()
    consts_f = nc.dram_tensor("consts_f", [128, 152], F32, kind="ExternalInput").ap()
    eye64 = nc.dram_tensor("eye64", [64, 64], BF16, kind="ExternalInput").ap()
    x1T = nc.dram_tensor("x1T", [NB, D, T], F32, kind="ExternalOutput").ap()
    if DEBUG_A:
        dbg_xrb = nc.dram_tensor("dbg_xrb", [128, KD * C], BF16, kind="ExternalOutput").ap()
        dbg_u = nc.dram_tensor("dbg_u", [128, KE * H], BF16, kind="ExternalOutput").ap()
        dbg_sz = nc.dram_tensor("dbg_sz", [128, KE * H], BF16, kind="ExternalOutput").ap()
        dbg_ucv = nc.dram_tensor("dbg_ucv", [128, KE * H], BF16, kind="ExternalOutput").ap()
        dbg_dt = nc.dram_tensor("dbg_dt", [128, KE * H], BF16, kind="ExternalOutput").ap()
        dbg_du = nc.dram_tensor("dbg_du", [128, KE * H], BF16, kind="ExternalOutput").ap()
        dbg_bcb = nc.dram_tensor("dbg_bcb", [64, H], BF16, kind="ExternalOutput").ap()
        dbg_yg = nc.dram_tensor("dbg_yg", [128, KE * H], BF16, kind="ExternalOutput").ap()
        dbg_xpj = nc.dram_tensor("dbg_xpj", [128, KE * 128], BF16, kind="ExternalOutput").ap()
        dbg_dtpj = nc.dram_tensor("dbg_dtpj", [64, DI], BF16, kind="ExternalOutput").ap()
        dbg_asb = nc.dram_tensor("dbg_asb", [128, KE * NST], F32, kind="ExternalOutput").ap()
        dbg_bbc = nc.dram_tensor("dbg_bbc", [128, H], BF16, kind="ExternalOutput").ap()
        dbg_cbc = nc.dram_tensor("dbg_cbc", [128, H], BF16, kind="ExternalOutput").ap()
        dbg_da = nc.dram_tensor("dbg_da", [128, H], F32, kind="ExternalOutput").ap()
        dbg_bb = nc.dram_tensor("dbg_bb", [128, H], BF16, kind="ExternalOutput").ap()
        dbg_hh = nc.dram_tensor("dbg_hh", [128, H], F32, kind="ExternalOutput").ap()
    win_s = nc.dram_tensor("win_s", [D, 2 * DI], BF16, kind="Internal").ap()
    rows_s = nc.dram_tensor("rows_s", [2, 2 * C], F32, kind="Internal").ap()
    opj_s = nc.dram_tensor("opj_s", [DI, D], BF16, kind="Internal").ap()

    from contextlib import ExitStack
    es = ExitStack()
    UCV = es.enter_context(nc.sbuf_tensor("UCV", [128, KE * H], BF16))
    DU = es.enter_context(nc.sbuf_tensor("DU", [128, KE * H], BF16))
    SZ = es.enter_context(nc.sbuf_tensor("SZ", [128, KE * H], BF16))
    DT = es.enter_context(nc.sbuf_tensor("DT", [128, KE * H], BF16))
    ARF = es.enter_context(nc.sbuf_tensor("ARF", [128, 4608], F32))
    ARB = es.enter_context(nc.sbuf_tensor("ARB", [128, 9216], BF16))
    YA = es.enter_context(nc.sbuf_tensor("YA", [128, 4 * H], F32))
    HST = es.enter_context(nc.sbuf_tensor("HST", [128, KE * NST], F32))
    XPJ = es.enter_context(nc.sbuf_tensor("XPJ", [128, KE * 128], BF16))
    DTPJ = es.enter_context(nc.sbuf_tensor("DTPJ", [64, DI], BF16))
    ASB = es.enter_context(nc.sbuf_tensor("ASB", [128, KE * NST], F32))
    COLS = es.enter_context(nc.sbuf_tensor("COLS", [128, KE * 4], F32))
    CW = es.enter_context(nc.sbuf_tensor("CW", [128, KE * 4], F32))
    CNB = es.enter_context(nc.sbuf_tensor("CNB", [128, 130], BF16))
    CNF = es.enter_context(nc.sbuf_tensor("CNF", [128, 152], F32))
    EYEB = es.enter_context(nc.sbuf_tensor("EYEB", [64, 64], BF16))
    DTL = es.enter_context(nc.sbuf_tensor("DTL", [64, H], BF16))
    BCB = es.enter_context(nc.sbuf_tensor("BCB", [64, H], BF16))
    CVH = es.enter_context(nc.sbuf_tensor("CVH", [128, KE * 3], BF16))
    RM = es.enter_context(nc.sbuf_tensor("RM", [1, C], F32))
    RE2 = es.enter_context(nc.sbuf_tensor("RE2", [1, C], F32))
    RV = es.enter_context(nc.sbuf_tensor("RV", [1, C], F32))
    RS = es.enter_context(nc.sbuf_tensor("RS", [1, C], F32))
    RR = es.enter_context(nc.sbuf_tensor("RR", [1, C], F32))

    PH0 = es.enter_context(nc.psum_tensor("PH0", [128, H], F32))
    PH1 = es.enter_context(nc.psum_tensor("PH1", [128, H], F32))
    PB = es.enter_context(nc.psum_tensor("PB", [128, H], F32))
    PC = es.enter_context(nc.psum_tensor("PC", [128, H], F32))
    P0 = PH0[:, 0:C]
    P1 = PH0[:, C:2 * C]
    P2 = PH1[:, 0:C]
    P3 = PH1[:, C:2 * C]
    sd = es.enter_context(nc.semaphore("sd"))
    sp = es.enter_context(nc.semaphore("sp"))
    sa = es.enter_context(nc.semaphore("sa"))
    sv = es.enter_context(nc.semaphore("sv"))
    sg = es.enter_context(nc.semaphore("sg"))
    if True:
        p = Prog(nc, {"sd": sd, "sp": sp, "sa": sa, "sv": sv, "sg": sg})
        PS = [P0, P1, P2, P3]

        XC = ARF[:, 0:4096]
        S1 = ARF[:, 4096:4608]
        S2 = ARF[:, 0:512]
        RT = S1[0:1, 0:C]
        DAv = [ARF[:, 0:1024], ARF[:, 1024:2048]]
        HHS = ARF[:, 2048:3072]
        CHS = ARF[:, 3072:4096]
        X1O = [ARF[:, 4096:4608], ARF[:, 4096:4608]]

        SQB = [ARB[:, 0:512], ARB[:, 512:1024]]
        XRB = ARB[:, 1024:5120]
        WIN = [ARB[:, 5120:7168], ARB[:, 7168:9216]]
        BBb = ARB[:, 0:1024]
        BBC = ARB[:, 1024:2048]
        CBC = ARB[:, 2048:3072]
        CVT = ARB[:, 3072:4100]
        WOP = [ARB[:, 5120:7168], ARB[:, 7168:9216]]

        ONES_COL_B = CNB[:, 0:1]
        ONES_ROW_F = CNF[0:1, 1:129]
        ONES_COL_F = CNF[:, 0:1]
        for _i, _v in enumerate(ACT_CONSTS):
            nc.const_aps.register(F32, float(_v), CNF[:, 129 + _i:130 + _i])

        p.dma(COLS[:], cols)
        p.dma(CW[:], convw)
        p.dma(CNB[:], consts_b)
        p.dma(CNF[:], consts_f)
        p.dma(EYEB[:], eye64)
        p.wait("scalar", "sd")
        p.wait("vector", "sd")
        p.wait("tensor", "sd")
        p.wait("gpsimd", "sd")
        p.dma(XC[:, 0:KE * 128].rearrange("p (g f) -> p g f", g=KE),
              xprojT.rearrange("(g p) f -> p g f", p=128))
        p.wait("vector", "sd")
        p.inc("vector", "sv", lambda e: e.tensor_copy(XPJ[:], XC[:, 0:KE * 128]))
        p.wait("sync", "sv")
        p.dma(XC[0:DTR, 0:DI], dtprojT)
        p.wait("vector", "sd")
        p.inc("vector", "sv", lambda e: e.tensor_copy(DTPJ[:], XC[0:DTR, 0:DI]))
        p.wait("sync", "sv")
        p.dma(XC[:, 0:KE * NST], alog_t)
        p.wait("scalar", "sd")
        p.raw("scalar", lambda e: e.activation(out=ASB[:], in_=XC[:, 0:KE * NST], func=AF.Exp))
        p.inc("scalar", "sa", lambda e: e.mul(ASB[:], ASB[:], -1.0))
        p.wait("sync", "sv")
        p.wait("sync", "sa")
        for (src, dst, nrows, ncols) in ((winT, win_s, D, 2 * DI), (outprojT, opj_s, DI, D)):
            for g in range(nrows // 128):
                p.wait("sync", "sd")
                p.dma(XC[:, 0:ncols], src[g * 128:(g + 1) * 128, :])
                p.wait("vector", "sd")
                p.inc("vector", "sv", lambda e, ncc=ncols: e.tensor_copy(ARB[:, 0:ncc], XC[:, 0:ncc]))
                p.wait("sync", "sv")
                p.dma(dst[g * 128:(g + 1) * 128, :], ARB[:, 0:ncols])
        p.barrier()

        for b in range(NB):
            p.raw("vector", lambda e: e.memset(CVH[:], 0))
            p.inc("vector", "sv", lambda e: e.memset(HST[:], 0))
            p.barrier()
            for hf in range(2):
                toff = hf * H
                for tc in range(2):
                    t0 = toff + tc * C
                    hc = tc * C
                    p.wait("sync", "sv")
                    p.wait("sync", "sp")
                    p.dma(XC.rearrange("p (g f) -> p g f", g=KD),
                          xT[b].rearrange("(g p) t -> p g t", p=128)[:, :, t0:t0 + C])
                    p.wait("tensor", "sd")
                    for k in range(KD):
                        p.inc("tensor", "sp", lambda e, k=k: e.matmul(
                            P0[0:1, :], ONES_COL_F, XC[:, k * C:(k + 1) * C],
                            start=(k == 0), stop=(k == KD - 1)))
                    sq_free = [0, 0]
                    p.wait("scalar", "sd")
                    for k in range(KD):
                        sl = k % 2
                        p.wait("scalar", "sp", sq_free[sl])
                        p.inc("scalar", "sa", lambda e, k=k, sl=sl: e.activation(
                            out=SQB[sl], in_=XC[:, k * C:(k + 1) * C], func=AF.Square))
                        p.wait("tensor", "sa")
                        sq_free[sl] = p.inc("tensor", "sp", lambda e, k=k, sl=sl: e.matmul(
                            P1[0:1, :], ONES_COL_B, SQB[sl],
                            start=(k == 0), stop=(k == KD - 1)))
                    p.wait("vector", "sp")
                    p.raw("vector", lambda e: e.tensor_scalar(out=RM[:], in0=P0[0:1, :], scalar1=1.0 / D, scalar2=None, op0=ALU.mult))
                    p.inc("vector", "sv", lambda e: e.tensor_scalar(out=RE2[:], in0=P1[0:1, :], scalar1=1.0 / D, scalar2=None, op0=ALU.mult))
                    p.wait("vector", "sv")
                    p.inc("vector", "sv", lambda e: e.tensor_tensor(out=RT[:], in0=RM[:], in1=RM[:], op=ALU.mult))
                    p.wait("vector", "sv")
                    p.inc("vector", "sv", lambda e: e.tensor_tensor(out=RV[:], in0=RE2[:], in1=RT[:], op=ALU.subtract))
                    p.wait("scalar", "sv")
                    p.inc("scalar", "sa", lambda e: e.activation(out=RT[:], in_=RV[:], func=AF.Ln, bias=1e-5))
                    p.wait("scalar", "sa")
                    p.inc("scalar", "sa", lambda e: e.activation(out=RS[:], in_=RT[:], func=AF.Exp, scale=-0.5))
                    p.wait("sync", "sa")
                    p.dma(rows_s[0:1, hc:hc + C], RM[:])
                    p.dma(rows_s[1:2, hc:hc + C], RS[:])
                    p.wait("vector", "sa")
                    p.inc("vector", "sv", lambda e: e.tensor_tensor(out=RT[:], in0=RS[:], in1=RS[:], op=ALU.mult))
                    p.wait("vector", "sv")
                    p.inc("vector", "sv", lambda e: e.tensor_tensor(out=RT[:], in0=RV[:], in1=RT[:], op=ALU.mult))
                    p.wait("scalar", "sv")
                    p.inc("scalar", "sa", lambda e: e.activation(out=RT[:], in_=RT[:], func=AF.Ln, bias=1e-6))
                    p.wait("scalar", "sa")
                    p.inc("scalar", "sa", lambda e: e.activation(out=RT[:], in_=RT[:], func=AF.Exp, scale=-0.5))
                    p.wait("vector", "sa")
                    p.inc("vector", "sv", lambda e: e.tensor_tensor(out=RR[:], in0=RS[:], in1=RT[:], op=ALU.mult))
                    p.wait("tensor", "sv")
                    p.inc("tensor", "sp", lambda e: e.matmul(P2[:, :], ONES_ROW_F, RM[0:1, :], start=True, stop=True))
                    p.inc("tensor", "sp", lambda e: e.matmul(P3[:, :], ONES_ROW_F, RR[0:1, :], start=True, stop=True))
                    p.wait("vector", "sp")
                    for k in range(KD):
                        p.raw("vector", lambda e, k=k: e.tensor_tensor(
                            out=S1[:, 0:C], in0=XC[:, k * C:(k + 1) * C], in1=P2[:, :], op=ALU.subtract))
                        p.inc("vector", "sv", lambda e, k=k: e.scalar_tensor_tensor(
                            out=XRB[:, k * C:(k + 1) * C], in0=S1[:, 0:C],
                            scalar=_col(COLS, k * 4 + 3), in1=P3[:, :], op0=ALU.mult, op1=ALU.mult))
                    xrb_ready = p.cnt["sv"]

                    win_used = [0, 0]
                    ps_free = [0, 0, 0, 0]
                    ps_sem = ["sv", "sv", "sv", "sv"]
                    for eb in range(16):
                        buf = eb % 2
                        p.wait("sync", "sp", win_used[buf])
                        p.dma(WIN[buf].rearrange("p (g f) -> p g f", g=KD),
                              win_s.rearrange("(g p) e -> p g e", p=128)[:, :, eb * 256:(eb + 1) * 256])
                        p.wait("tensor", "sd")
                        p.wait("tensor", "sv", xrb_ready)
                        for e2 in range(2):
                            eg = eb * 2 + e2
                            ps_i = eg % 4
                            p.wait("tensor", ps_sem[ps_i], ps_free[ps_i])
                            for k in range(KD):
                                p.inc("tensor", "sp", lambda e, buf=buf, k=k, e2=e2, ps_i=ps_i: e.matmul(
                                    PS[ps_i][:, :], WIN[buf][:, k * 256 + e2 * 128:k * 256 + (e2 + 1) * 128],
                                    XRB[:, k * C:(k + 1) * C], start=(k == 0), stop=(k == KD - 1)))
                            win_used[buf] = p.cnt["sp"]
                            p.wait("scalar", "sp")
                            if eg < KE:
                                dstc = eg * H + hc
                                p.inc("scalar", "sa", lambda e, ps_i=ps_i, dstc=dstc: e.copy(
                                    out=UCV[:, dstc:dstc + C], in_=PS[ps_i][:, :]))
                            else:
                                dstc = (eg - KE) * H + hc
                                p.inc("scalar", "sa", lambda e, ps_i=ps_i, dstc=dstc: e.activation(
                                    out=SZ[:, dstc:dstc + C], in_=PS[ps_i][:, :], func=AF.Silu))
                            ps_free[ps_i] = p.cnt["sa"]
                            ps_sem[ps_i] = "sa"
                    if DEBUG_A and b == 0 and hf == 0 and tc == 0:
                        p.dma(dbg_xrb, XRB[:])
                    p.barrier()
                if DEBUG_A and b == 0 and hf == 0:
                    if True:
                        p.dma(dbg_u, UCV[:])
                        p.dma(dbg_sz, SZ[:])
                        p.barrier()

                for k in range(KE):
                    p.raw("vector", lambda e, k=k: e.tensor_copy(CVT[:, 0:3], CVH[:, k * 3:(k + 1) * 3]))
                    p.raw("vector", lambda e, k=k: e.tensor_copy(CVT[:, 3:H + 3], UCV[:, k * H:(k + 1) * H]))
                    p.raw("vector", lambda e, k=k: e.tensor_copy(CVH[:, k * 3:(k + 1) * 3], CVT[:, H:H + 3]))
                    p.raw("vector", lambda e, k=k: e.tensor_scalar(
                        out=CHS[:], in0=CVT[:, 0:H], scalar1=_col(CW, k * 4 + 0), scalar2=None, op0=ALU.mult))
                    for j in range(1, 4):
                        fn = lambda e, k=k, j=j: e.scalar_tensor_tensor(
                            out=CHS[:], in0=CVT[:, j:j + H], scalar=_col(CW, k * 4 + j), in1=CHS[:],
                            op0=ALU.mult, op1=ALU.add)
                        if j == 3:
                            p.inc("vector", "sv", fn)
                        else:
                            p.raw("vector", fn)
                    p.wait("scalar", "sv")
                    p.inc("scalar", "sa", lambda e, k=k: e.activation(
                        out=UCV[:, k * H:(k + 1) * H], in_=CHS[:], func=AF.Silu, bias=_col(COLS, k * 4 + 0)))
                    p.wait("vector", "sa")
                p.barrier()
                if DEBUG_A and b == 0 and hf == 0:
                    p.dma(dbg_ucv, UCV[:])
                    p.barrier()

                dt_ps_free = {1: 0, 2: 0}
                for tc in range(2):
                    hc = tc * C
                    p.wait("tensor", "sv")
                    for k in range(KE):
                        p.inc("tensor", "sp", lambda e, k=k, hc=hc: e.matmul(
                            P0[:, :], XPJ[:, k * 128:(k + 1) * 128],
                            UCV[:, k * H + hc:k * H + hc + C], start=(k == 0), stop=(k == KE - 1)))
                    p.wait("vector", "sp")
                    p.raw("vector", lambda e, hc=hc: e.tensor_copy(DTL[0:64, hc:hc + C], P0[0:64, :]))
                    p.raw("vector", lambda e, hc=hc: e.tensor_copy(BCB[0:32, hc:hc + C], P0[64:96, :]))
                    p.inc("vector", "sv", lambda e, hc=hc: e.tensor_copy(BCB[32:64, hc:hc + C], P0[96:128, :]))
                    p.wait("tensor", "sv")
                    for k in range(KE):
                        ps_i = 1 + (k % 2)
                        p.wait("tensor", "sa", dt_ps_free[ps_i])
                        p.inc("tensor", "sp", lambda e, k=k, hc=hc, ps_i=ps_i: e.matmul(
                            PS[ps_i][:, :], DTPJ[0:64, k * 128:(k + 1) * 128],
                            DTL[0:64, hc:hc + C], start=True, stop=True))
                        p.wait("scalar", "sp")
                        p.wait("scalar", "sv")
                        p.raw("scalar", lambda e, k=k, ps_i=ps_i: e.activation(
                            out=S1[:, 0:C], in_=PS[ps_i][:, :], func=AF.Abs, bias=_col(COLS, k * 4 + 1)))
                        p.raw("scalar", lambda e: e.activation(out=S2[:, 0:C], in_=S1[:, 0:C], func=AF.Exp, scale=-1.0))
                        p.raw("scalar", lambda e: e.activation(out=S1[:, 0:C], in_=S2[:, 0:C], func=AF.Ln, bias=1.0))
                        p.inc("scalar", "sa", lambda e, k=k, ps_i=ps_i: e.activation(
                            out=S2[:, 0:C], in_=PS[ps_i][:, :], func=AF.Relu, bias=_col(COLS, k * 4 + 1)))
                        dt_ps_free[ps_i] = p.cnt["sa"]
                        p.wait("vector", "sa")
                        p.inc("vector", "sv", lambda e, k=k, hc=hc: e.tensor_tensor(
                            out=DT[:, k * H + hc:k * H + hc + C], in0=S1[:, 0:C], in1=S2[:, 0:C], op=ALU.add))
                p.barrier()
                if DEBUG_A and b == 0 and hf == 0:
                    p.dma(dbg_dt, DT[:])
                    p.dma(dbg_bcb, BCB[:])
                    p.barrier()

                for k in range(KE):
                    p.inc("vector", "sv", lambda e, k=k: e.tensor_tensor(
                        out=DU[:, k * H:(k + 1) * H], in0=DT[:, k * H:(k + 1) * H],
                        in1=UCV[:, k * H:(k + 1) * H], op=ALU.mult))
                p.barrier()
                if DEBUG_A and b == 0 and hf == 0:
                    p.dma(dbg_du, DU[:])
                    p.barrier()

                for dh in range(4):
                    for n in range(NST):
                        p.wait("tensor", "sv")
                        for cc in range(2):
                            p.inc("tensor", "sp", lambda e, n=n, cc=cc: e.matmul(
                                PB[:, cc * C:(cc + 1) * C], EYEB[:, n:n + 1].broadcast_to([64, 128]),
                                BCB[0:64, cc * C:(cc + 1) * C], start=True, stop=True))
                            p.inc("tensor", "sp", lambda e, n=n, cc=cc: e.matmul(
                                PC[:, cc * C:(cc + 1) * C], EYEB[:, 32 + n:33 + n].broadcast_to([64, 128]),
                                BCB[0:64, cc * C:(cc + 1) * C], start=True, stop=True))
                        p.wait("vector", "sp")
                        p.raw("vector", lambda e: e.tensor_copy(BBC[:], PB[:, :]))
                        p.inc("vector", "sv", lambda e: e.tensor_copy(CBC[:], PC[:, :]))
                        for k8 in range(4):
                            k = dh * 4 + k8
                            par = k8 % 2
                            p.wait("scalar", "sv")
                            p.inc("scalar", "sa", lambda e, k=k, n=n, par=par: e.activation(
                                out=DAv[par], in_=DT[:, k * H:(k + 1) * H], func=AF.Exp,
                                scale=ASB[:, k * NST + n:k * NST + n + 1]))
                            p.raw("vector", lambda e, k=k: e.tensor_tensor(
                                out=BBb[:], in0=DU[:, k * H:(k + 1) * H], in1=BBC[:], op=ALU.mult))
                            p.wait("vector", "sa")
                            p.wait("vector", "sg")
                            init = 0.0 if hf == 0 else HST[:, k * NST + n:k * NST + n + 1]
                            p.raw("vector", lambda e, par=par, init=init: e.tensor_tensor_scan(
                                out=HHS[:], data0=DAv[par], data1=BBb[:], initial=init,
                                op0=ALU.mult, op1=ALU.add))
                            if n == 0:
                                p.inc("vector", "sv", lambda e, k8=k8: e.tensor_tensor(
                                    out=YA[:, k8 * H:(k8 + 1) * H], in0=HHS[:], in1=CBC[:], op=ALU.mult))
                            else:
                                p.raw("vector", lambda e: e.tensor_tensor(
                                    out=CHS[:], in0=HHS[:], in1=CBC[:], op=ALU.mult))
                                p.inc("vector", "sv", lambda e, k8=k8: e.tensor_tensor(
                                    out=YA[:, k8 * H:(k8 + 1) * H], in0=YA[:, k8 * H:(k8 + 1) * H],
                                    in1=CHS[:], op=ALU.add))
                            if hf == 0:
                                p.wait("gpsimd", "sv")
                                p.inc("gpsimd", "sg", lambda e, k=k, n=n: e.tensor_copy(
                                    HST[:, k * NST + n:k * NST + n + 1], HHS[:, H - 1:H]))
                            if DEBUG_A and b == 0 and hf == 0 and dh == 0 and n == 5 and k8 == 0:
                                p.barrier()
                                p.dma(dbg_asb, ASB[:])
                                p.dma(dbg_bbc, BBC[:])
                                p.dma(dbg_cbc, CBC[:])
                                p.dma(dbg_da, DAv[par])
                                p.dma(dbg_bb, BBb[:])
                                p.dma(dbg_hh, HHS[:])
                                p.barrier()
                    for k8 in range(4):
                        k = dh * 4 + k8
                        p.raw("vector", lambda e, k=k, k8=k8: e.scalar_tensor_tensor(
                            out=CHS[:], in0=UCV[:, k * H:(k + 1) * H], scalar=_col(COLS, k * 4 + 2),
                            in1=YA[:, k8 * H:(k8 + 1) * H], op0=ALU.mult, op1=ALU.add))
                        p.inc("vector", "sv", lambda e, k=k: e.tensor_tensor(
                            out=UCV[:, k * H:(k + 1) * H], in0=CHS[:], in1=SZ[:, k * H:(k + 1) * H],
                            op=ALU.mult))
                p.barrier()
                if DEBUG_A and b == 0 and hf == 0:
                    p.dma(dbg_yg, UCV[:])
                    p.barrier()

                wop_used = [0, 0]
                ps_free2 = [0, 0]
                XCS = [ARF[:, 0:C], ARF[:, C:2 * C]]
                XNS = ARF[:, 2 * C:3 * C]
                X1S = [ARF[:, 3 * C:4 * C], ARF[:, 4 * C:5 * C]]
                for tc in range(2):
                    t0 = toff + tc * C
                    hc = tc * C
                    p.dma(RM[:], rows_s[0:1, hc:hc + C])
                    p.dma(RS[:], rows_s[1:2, hc:hc + C])
                    p.wait("tensor", "sv")
                    p.wait("tensor", "sd")
                    p.inc("tensor", "sp", lambda e: e.matmul(
                        PB[:, 0:C], ONES_ROW_F, RM[0:1, :], start=True, stop=True))
                    p.inc("tensor", "sp", lambda e: e.matmul(
                        PC[:, 0:C], ONES_ROW_F, RS[0:1, :], start=True, stop=True))
                    for dm in range(KD):
                        buf = dm % 2
                        p.wait("sync", "sv")
                        p.dma(XCS[buf], xT[b][dm * 128:(dm + 1) * 128, t0:t0 + C])
                        p.wait("sync", "sp", wop_used[buf])
                        p.dma(WOP[buf].rearrange("p (g f) -> p g f", g=KE),
                              opj_s.rearrange("(g p) d -> p g d", p=128)[:, :, dm * 128:(dm + 1) * 128])
                        ps_i = dm % 2
                        p.wait("tensor", "sv", ps_free2[ps_i])
                        p.wait("tensor", "sd")
                        for k in range(KE):
                            p.inc("tensor", "sp", lambda e, k=k, buf=buf, hc=hc, ps_i=ps_i: e.matmul(
                                PS[ps_i][:, :], WOP[buf][:, k * 128:(k + 1) * 128],
                                UCV[:, k * H + hc:k * H + hc + C], start=(k == 0), stop=(k == KE - 1)))
                        wop_used[buf] = p.cnt["sp"]
                        p.wait("vector", "sp")
                        p.wait("vector", "sd")
                        ob = dm % 2
                        p.raw("vector", lambda e, buf=buf: e.tensor_tensor(
                            out=XNS, in0=XCS[buf], in1=PB[:, 0:C], op=ALU.subtract))
                        p.raw("vector", lambda e: e.tensor_tensor(
                            out=XNS, in0=XNS, in1=PC[:, 0:C], op=ALU.mult))
                        p.raw("vector", lambda e, ps_i=ps_i, ob=ob, buf=buf: e.tensor_tensor(
                            out=X1S[ob], in0=PS[ps_i][:, :], in1=XCS[buf], op=ALU.add))
                        p.inc("vector", "sv", lambda e, ob=ob: e.tensor_tensor(
                            out=X1S[ob], in0=X1S[ob], in1=XNS, op=ALU.add))
                        ps_free2[ps_i] = p.cnt["sv"]
                        p.wait("sync", "sv")
                        p.dma(x1T[b][dm * 128:(dm + 1) * 128, t0:t0 + C], X1S[ob])
                p.barrier()
        p.wait("sync", "sd")
        p.build_block()
        es.close()
    return nc



def _tile_cols(v, ke=KE):
    return np.ascontiguousarray(np.asarray(v).reshape(ke, 128).T)


def prep_phase_a_inputs(inputs):
    x = np.asarray(inputs["L0_attn_x"])
    win = np.asarray(inputs["L0_attn_in_proj_w"])
    convw = np.asarray(inputs["L0_attn_conv_w"])
    convb = np.asarray(inputs["L0_attn_conv_b"])
    xproj = np.asarray(inputs["L0_attn_x_proj_w"])
    dtproj = np.asarray(inputs["L0_attn_dt_proj_w"])
    dtb = np.asarray(inputs["L0_attn_dt_proj_b"])
    alog = np.asarray(inputs["L0_attn_A_log"])
    dpar = np.asarray(inputs["L0_attn_D_param"])
    outproj = np.asarray(inputs["L0_attn_out_proj_w"])
    normw = np.asarray(inputs["L0_attn_norm_w"])

    winT = np.ascontiguousarray(win.T)
    xprojT = np.ascontiguousarray(xproj.T)
    dtprojT = np.ascontiguousarray(dtproj.T)
    outprojT = np.ascontiguousarray(outproj.T)
    alog_t = np.ascontiguousarray(
        alog.reshape(KE, 128, NST).transpose(1, 0, 2).reshape(128, KE * NST))
    cols = np.zeros((128, KE * 4), np.float32)
    cols[:, 0::4] = _tile_cols(convb)
    cols[:, 1::4] = _tile_cols(dtb)
    cols[:, 2::4] = _tile_cols(dpar)
    nw = _tile_cols(normw, KD)
    for k in range(KD):
        cols[:, k * 4 + 3] = nw[:, k]
    convw_t = np.zeros((128, KE * 4), np.float32)
    for k in range(KE):
        convw_t[:, k * 4:(k + 1) * 4] = convw[k * 128:(k + 1) * 128, :]
    consts_b = np.zeros((128, 130), dtype=BF)
    consts_b[:, 0] = 1
    consts_b[0, 1:129] = 1
    consts_f = np.zeros((128, 152), np.float32)
    consts_f[:, 0] = 1
    consts_f[0, 1:129] = 1
    for _i, _v in enumerate(ACT_CONSTS):
        consts_f[:, 129 + _i] = _v
    eye64 = np.eye(64, dtype=BF)

    shared = dict(winT=winT, xprojT=xprojT, dtprojT=dtprojT, outprojT=outprojT,
                  alog_t=alog_t, cols=cols, convw=convw_t, consts_b=consts_b,
                  consts_f=consts_f, eye64=eye64)
    maps = []
    for c_ in range(8):
        xb = x[c_ * NB:(c_ + 1) * NB]
        m = dict(shared)
        m["xT"] = np.ascontiguousarray(xb.transpose(0, 2, 1))
        maps.append(m)
    return maps


def run_phase_a(inputs):
    nc = build_phase_a()
    maps = prep_phase_a_inputs(inputs)
    res = run_bass_kernel_spmd(nc, maps, core_ids=list(range(8)))
    x1 = np.empty((16, T, D), np.float32)
    for c_ in range(8):
        x1[c_ * NB:(c_ + 1) * NB] = res.results[c_]["x1T"].transpose(0, 2, 1)
    return x1


def kernel(**inputs):
    return run_fused(inputs)



TSGN = [(4, 1.0), (5, 1.0), (6, -1.0), (7, -1.0), (0, -1.0), (1, -1.0), (2, 1.0), (3, 1.0)]


def build_phase_b():
    nc = bass.Bass("TRN2", target_bir_lowering=False, debug=False)
    x1T = nc.dram_tensor("x1T", [NB, D, T], F32, kind="ExternalInput").ap()
    cn_w1 = nc.dram_tensor("cn_w1", [D, D], F32, kind="ExternalInput").ap()
    cn_w2 = nc.dram_tensor("cn_w2", [D, 512], F32, kind="ExternalInput").ap()
    cn_w3 = nc.dram_tensor("cn_w3", [512, 256], F32, kind="ExternalInput").ap()
    cn_w4 = nc.dram_tensor("cn_w4", [256, 1], F32, kind="ExternalInput").ap()
    gc_w1 = nc.dram_tensor("gc_w1", [D, 512], F32, kind="ExternalInput").ap()
    gc_w2 = nc.dram_tensor("gc_w2", [512, 256], F32, kind="ExternalInput").ap()
    gc_w3 = nc.dram_tensor("gc_w3", [256, 1], F32, kind="ExternalInput").ap()
    wv = nc.dram_tensor("wv", [D, D], F32, kind="ExternalInput").ap()
    wout = nc.dram_tensor("wout", [D, D], F32, kind="ExternalInput").ap()
    wproj = nc.dram_tensor("wproj", [D, D], F32, kind="ExternalInput").ap()
    woffx = nc.dram_tensor("woffx", [D, 128], F32, kind="ExternalInput").ap()
    woffy = nc.dram_tensor("woffy", [D, 128], F32, kind="ExternalInput").ap()
    wattn = nc.dram_tensor("wattn", [D, 128], F32, kind="ExternalInput").ap()
    w1T = nc.dram_tensor("w1T", [D, 4 * D], F32, kind="ExternalInput").ap()
    w2T = nc.dram_tensor("w2T", [4 * D, D], F32, kind="ExternalInput").ap()
    colb = nc.dram_tensor("colb", [128, 88], F32, kind="ExternalInput").ap()
    crows = nc.dram_tensor("crows", [1, 8], F32, kind="ExternalInput").ap()
    refm = nc.dram_tensor("refm", [128, T], F32, kind="ExternalInput").ap()
    selc = nc.dram_tensor("selc", [128, 4 * 141], BF16, kind="ExternalInput").ap()
    eye128 = nc.dram_tensor("eye128", [128, 128], BF16, kind="ExternalInput").ap()
    savg = nc.dram_tensor("savg", [128, 128], F32, kind="ExternalInput").ap()
    ssumb = nc.dram_tensor("ssumb", [128, 128], BF16, kind="ExternalInput").ap()
    consts_b = nc.dram_tensor("consts_b", [128, 130], BF16, kind="ExternalInput").ap()
    consts_f = nc.dram_tensor("consts_f", [128, 152], F32, kind="ExternalInput").ap()
    yT = nc.dram_tensor("yT", [NB, D, T], F32, kind="ExternalOutput").ap()
    if DEBUG_B:
        dbg_thc = nc.dram_tensor("dbg_thc", [128, 4], F32, kind="ExternalOutput").ap()
        dbg_x2 = nc.dram_tensor("dbg_x2", [D, T], F32, kind="ExternalOutput").ap()
        dbg_x3 = nc.dram_tensor("dbg_x3", [D, T], F32, kind="ExternalOutput").ap()
        dbg_aw = nc.dram_tensor("dbg_aw", [128, C], F32, kind="ExternalOutput").ap()
        dbg_aj = nc.dram_tensor("dbg_aj", [128, 2048], BF16, kind="ExternalOutput").ap()
        dbg_vtd = nc.dram_tensor("dbg_vtd", [128, 1024], BF16, kind="ExternalOutput").ap()
        dbg_gq = nc.dram_tensor("dbg_gq", [128, C], BF16, kind="ExternalOutput").ap()
        dbg_rsm = nc.dram_tensor("dbg_rsm", [1, 16], F32, kind="ExternalOutput").ap()
        dbg_tfr = nc.dram_tensor("dbg_tfr", [1, C], F32, kind="ExternalOutput").ap()
        dbg_h1 = nc.dram_tensor("dbg_h1", [128, 4096], BF16, kind="ExternalOutput").ap()
        dbg_h2 = nc.dram_tensor("dbg_h2", [128, 2048], BF16, kind="ExternalOutput").ap()
        dbg_h3 = nc.dram_tensor("dbg_h3", [128, 1024], BF16, kind="ExternalOutput").ap()
        dbg_oa = nc.dram_tensor("dbg_oa", [128, 4096], BF16, kind="ExternalOutput").ap()
        dbg_wq1 = nc.dram_tensor("dbg_wq1", [128, 4096], BF16, kind="ExternalOutput").ap()
        dbg_hh1 = nc.dram_tensor("dbg_hh1", [128, 16384], BF16, kind="ExternalOutput").ap()
    wq1_s = nc.dram_tensor("wq1_s", [D, 4 * D], BF16, kind="Internal").ap()
    wq2_s = nc.dram_tensor("wq2_s", [4 * D, D], BF16, kind="Internal").ap()
    wv_s = nc.dram_tensor("wv_s", [D, D], BF16, kind="Internal").ap()
    wout_s = nc.dram_tensor("wout_s", [D, D], BF16, kind="Internal").ap()
    wproj_s = nc.dram_tensor("wproj_s", [D, D], BF16, kind="Internal").ap()

    from contextlib import ExitStack
    es = ExitStack()
    XF = es.enter_context(nc.sbuf_tensor("XF", [128, 4096], F32))
    OA = es.enter_context(nc.sbuf_tensor("OA", [128, 4096], BF16))
    OB = es.enter_context(nc.sbuf_tensor("OB", [128, 4096], BF16))
    XNB = es.enter_context(nc.sbuf_tensor("XNB", [128, 4096], BF16))
    HB = es.enter_context(nc.sbuf_tensor("HB", [128, 16384], BF16))
    WS = es.enter_context(nc.sbuf_tensor("WS", [128, 8192], BF16))
    CN1 = es.enter_context(nc.sbuf_tensor("CN1", [128, 8192], BF16))
    CN2 = es.enter_context(nc.sbuf_tensor("CN2", [128, 4096], BF16))
    CN3 = es.enter_context(nc.sbuf_tensor("CN3", [128, 1024], BF16))
    CN4 = es.enter_context(nc.sbuf_tensor("CN4", [128, 2], BF16))
    GC1 = es.enter_context(nc.sbuf_tensor("GC1", [128, 4096], BF16))
    GC2 = es.enter_context(nc.sbuf_tensor("GC2", [128, 1024], BF16))
    GC3 = es.enter_context(nc.sbuf_tensor("GC3", [128, 2], BF16))
    WOX = es.enter_context(nc.sbuf_tensor("WOX", [128, 1024], BF16))
    WOY = es.enter_context(nc.sbuf_tensor("WOY", [128, 1024], BF16))
    WAT = es.enter_context(nc.sbuf_tensor("WAT", [128, 1024], BF16))
    SELC = es.enter_context(nc.sbuf_tensor("SELC", [128, 4 * 141], BF16))
    EYE = es.enter_context(nc.sbuf_tensor("EYE", [128, 128], BF16))
    SAVG = es.enter_context(nc.sbuf_tensor("SAVG", [128, 128], F32))
    SSUM = es.enter_context(nc.sbuf_tensor("SSUM", [128, 128], BF16))
    EB = es.enter_context(nc.sbuf_tensor("EB", [128, C], BF16))
    COLB = es.enter_context(nc.sbuf_tensor("COLB", [128, 88], F32))
    CROW = es.enter_context(nc.sbuf_tensor("CROW", [1, 8], F32))
    GV = es.enter_context(nc.sbuf_tensor("GV", [128, 8], F32))
    GVB = es.enter_context(nc.sbuf_tensor("GVB", [128, 8], BF16))
    H1G = es.enter_context(nc.sbuf_tensor("H1G", [128, 4], BF16))
    H2G = es.enter_context(nc.sbuf_tensor("H2G", [128, 2], BF16))
    VAL = es.enter_context(nc.sbuf_tensor("VAL", [128, 112], BF16))
    VTD = es.enter_context(nc.sbuf_tensor("VTD", [128, 1024], BF16))
    AWL = es.enter_context(nc.sbuf_tensor("AWL", [128, C], F32))
    M4 = es.enter_context(nc.sbuf_tensor("M4", [128, C], F32))
    IX = es.enter_context(nc.sbuf_tensor("IX", [128, C], F32))
    IY = es.enter_context(nc.sbuf_tensor("IY", [128, C], F32))
    GB = es.enter_context(nc.sbuf_tensor("GB", [128, C], BF16))
    HAT = es.enter_context(nc.sbuf_tensor("HAT", [128, C], F32))
    MSK1 = es.enter_context(nc.sbuf_tensor("MSK1", [128, C], BF16))
    MSK2 = es.enter_context(nc.sbuf_tensor("MSK2", [128, C], BF16))
    AJ = es.enter_context(nc.sbuf_tensor("AJ", [128, 2048], BF16))
    REFC = es.enter_context(nc.sbuf_tensor("REFC", [128, C], F32))
    SCR = es.enter_context(nc.sbuf_tensor("SCR", [128, C], F32))
    CNB = es.enter_context(nc.sbuf_tensor("CNB", [128, 130], BF16))
    CNF = es.enter_context(nc.sbuf_tensor("CNF", [128, 152], F32))
    RM = es.enter_context(nc.sbuf_tensor("RM", [1, C], F32))
    RE2 = es.enter_context(nc.sbuf_tensor("RE2", [1, C], F32))
    RV = es.enter_context(nc.sbuf_tensor("RV", [1, C], F32))
    RS = es.enter_context(nc.sbuf_tensor("RS", [1, C], F32))
    TFR = es.enter_context(nc.sbuf_tensor("TFR", [1, C], F32))
    CPR = es.enter_context(nc.sbuf_tensor("CPR", [1, C], F32))
    RSM = es.enter_context(nc.sbuf_tensor("RSM", [1, 16], F32))
    THC = es.enter_context(nc.sbuf_tensor("THC", [128, 4], F32))
    ACC = es.enter_context(nc.sbuf_tensor("ACC", [128, 2], F32))
    Q0 = es.enter_context(nc.psum_tensor("Q0", [128, C], F32))
    Q1 = es.enter_context(nc.psum_tensor("Q1", [128, C], F32))
    Q2 = es.enter_context(nc.psum_tensor("Q2", [128, C], F32))
    Q3 = es.enter_context(nc.psum_tensor("Q3", [128, C], F32))
    Q4 = es.enter_context(nc.psum_tensor("Q4", [128, C], F32))
    Q5 = es.enter_context(nc.psum_tensor("Q5", [128, C], F32))
    PT = es.enter_context(nc.psum_tensor("PT", [128, 128], BF16))
    sd = es.enter_context(nc.semaphore("sd"))
    sp = es.enter_context(nc.semaphore("sp"))
    sa = es.enter_context(nc.semaphore("sa"))
    sv = es.enter_context(nc.semaphore("sv"))
    sg = es.enter_context(nc.semaphore("sg"))
    if True:
        p = Prog(nc, {"sd": sd, "sp": sp, "sa": sa, "sv": sv, "sg": sg})
        QS = [Q0, Q1, Q2, Q3, Q4, Q5]
        qfree = {}

        def qwait(i):
            sem, cnt = qfree.get(i, ("sv", 0))
            p.wait("tensor", sem, cnt)

        def qrel(i, sem):
            qfree[i] = (sem, p.cnt[sem])

        def qreset():
            qfree.clear()

        H1B = HB[:, 0:4096]
        H2B = HB[:, 4096:6144]
        H3B = HB[:, 6144:7168]
        SQ = [HB[:, 7168:7680], HB[:, 7680:8192]]
        ATN = OA
        O1B = OB
        WSa = WS[:, 0:4096]
        WSb = WS[:, 4096:8192]

        ONES_COL_B = CNB[:, 0:1]
        ONES_ROW_F = CNF[0:1, 1:129]
        ONES_COL_F = CNF[:, 0:1]
        for _i, _v in enumerate(ACT_CONSTS):
            nc.const_aps.register(F32, float(_v), CNF[:, 129 + _i:130 + _i])

        bv_c = lambda pt: _col(COLB, 0 + pt)
        bout_c = lambda pt: _col(COLB, 8 + pt)
        bproj_c = lambda pt: _col(COLB, 16 + pt)
        cnb1_c = lambda pt: _col(COLB, 24 + pt)
        cnb2_c = lambda pt: _col(COLB, 32 + pt)
        cnb3_c = lambda pt: _col(COLB, 36 + pt)
        gcb1_c = lambda pt: _col(COLB, 38 + pt)
        gcb2_c = lambda pt: _col(COLB, 42 + pt)
        batt_c = _col(COLB, 44)
        boffx_c = _col(COLB, 45)
        boffy_c = _col(COLB, 46)
        b1_c = lambda et: _col(COLB, 48 + et)
        b2_c = lambda pt: _col(COLB, 80 + pt)

        p.dma(COLB[:], colb)
        p.dma(CROW[:], crows)
        p.dma(CNB[:], consts_b)
        p.dma(CNF[:], consts_f)
        p.dma(SELC[:], selc)
        p.dma(EYE[:], eye128)
        p.dma(SAVG[:], savg)
        p.dma(SSUM[:], ssumb)
        for e_ in Prog.ENGINES:
            p.wait(e_, "sd")

        def load_w(srcap, dst, nrows, ncols, dst_stride):
            for g in range(nrows // 128):
                p.wait("sync", "sv")
                p.dma(XF[:, 0:ncols], srcap[g * 128:(g + 1) * 128, :])
                p.wait("vector", "sd")
                p.inc("vector", "sv", lambda e, g=g, ncols=ncols, dst=dst, ds=dst_stride: e.tensor_copy(
                    dst[:, g * ds:g * ds + ncols], XF[:, 0:ncols]))

        load_w(cn_w1, CN1, D, 1024, 1024)
        load_w(cn_w2, CN2, D, 512, 512)
        load_w(cn_w3, CN3, 512, 256, 256)
        load_w(cn_w4, CN4, 256, 1, 1)
        load_w(gc_w1, GC1, D, 512, 512)
        load_w(gc_w2, GC2, 512, 256, 256)
        load_w(gc_w3, GC3, 256, 1, 1)
        for (asrc, adst) in ((wv, wv_s), (wout, wout_s), (wproj, wproj_s)):
            for g in range(KD):
                p.wait("sync", "sv")
                p.wait("sync", "sd")
                p.dma(XF[:, 0:1024], asrc[g * 128:(g + 1) * 128, :])
                p.wait("vector", "sd")
                p.inc("vector", "sv", lambda e: e.tensor_copy(HB[:, 0:1024], XF[:, 0:1024]))
                p.wait("sync", "sv")
                p.dma(adst[g * 128:(g + 1) * 128, :], HB[:, 0:1024])
        load_w(woffx, WOX, D, 128, 128)
        load_w(woffy, WOY, D, 128, 128)
        load_w(wattn, WAT, D, 128, 128)
        p.raw("vector", lambda e: e.tensor_scalar(out=RSM[0:1, 1:2], in0=CROW[0:1, 1:2], scalar1=0.5, scalar2=None, op0=ALU.mult))
        p.inc("vector", "sv", lambda e: e.tensor_scalar(out=RSM[0:1, 2:3], in0=CROW[0:1, 2:3], scalar1=0.5, scalar2=None, op0=ALU.mult))
        p.barrier()

        for (wsrc, wdst, ngr, ncols, thi) in ((w1T, wq1_s, 8, 4096, 0), (w2T, wq2_s, 32, 1024, 2)):
            cs = min(ncols, 2048)
            qtiles = [(g, j) for g in range(ngr) for j in range(ncols // cs)]
            XQB = [XF[:, 0:cs], XF[:, cs:2 * cs]]
            ldq = [0, 0]
            rdq = [0, 0]
            p.wait("sync", "sv")
            p.wait("sync", "sd")
            p.inc("vector", "sv", lambda e: e.memset(ACC[:, 0:1], 0))
            p.wait("vector", "sv")
            qg0, qj0 = qtiles[0]
            ldq[0] = p.dma(XQB[0], wsrc[qg0 * 128:(qg0 + 1) * 128, qj0 * cs:(qj0 + 1) * cs])
            for idx, (g, j) in enumerate(qtiles):
                b = idx % 2
                if idx + 1 < len(qtiles):
                    nb = (idx + 1) % 2
                    ng, nj = qtiles[idx + 1]
                    p.wait("sync", "sv", rdq[nb])
                    ldq[nb] = p.dma(XQB[nb], wsrc[ng * 128:(ng + 1) * 128, nj * cs:(nj + 1) * cs])
                p.wait("vector", "sd", ldq[b])
                rdq[b] = p.inc("vector", "sv", lambda e, i=XQB[b]: e.tensor_reduce(
                    out=ACC[:, 1:2], in_=i, axis=mybir.AxisListType.X,
                    op=ALU.add, apply_absolute_value=True))
                p.wait("vector", "sv")
                p.inc("vector", "sv", lambda e: e.tensor_tensor(
                    out=ACC[:, 0:1], in0=ACC[:, 0:1], in1=ACC[:, 1:2], op=ALU.add))
                p.wait("vector", "sv")
            p.wait("tensor", "sv")
            p.inc("tensor", "sp", lambda e: e.matmul(Q0[0:1, 0:1], ONES_COL_F, ACC[:, 0:1], start=True, stop=True))
            p.wait("vector", "sp")
            nrm = 1.0 / (ngr * 128 * ncols)
            p.inc("vector", "sv", lambda e, nrm=nrm: e.tensor_scalar(
                out=RSM[0:1, 4:5], in0=Q0[0:1, 0:1], scalar1=0.5 * nrm, scalar2=0.5e-5,
                op0=ALU.mult, op1=ALU.add))
            p.wait("tensor", "sv")
            p.inc("tensor", "sp", lambda e: e.matmul(Q1[:, 0:1], ONES_ROW_F, RSM[0:1, 4:5], start=True, stop=True))
            p.wait("vector", "sp")
            p.raw("vector", lambda e, thi=thi: e.tensor_copy(THC[:, thi:thi + 1], Q1[:, 0:1]))
            p.inc("vector", "sv", lambda e, thi=thi: e.tensor_scalar(
                out=THC[:, thi + 1:thi + 2], in0=Q1[:, 0:1], scalar1=-1.0, scalar2=None, op0=ALU.mult))
            p.wait("vector", "sv")
            p.wait("sync", "sv", rdq[0])
            ldq[0] = p.dma(XQB[0], wsrc[qg0 * 128:(qg0 + 1) * 128, qj0 * cs:(qj0 + 1) * cs])
            for idx, (g, j) in enumerate(qtiles):
                b = idx % 2
                if idx + 1 < len(qtiles):
                    nb = (idx + 1) % 2
                    ng, nj = qtiles[idx + 1]
                    p.wait("sync", "sv", rdq[nb])
                    ldq[nb] = p.dma(XQB[nb], wsrc[ng * 128:(ng + 1) * 128, nj * cs:(nj + 1) * cs])
                p.wait("vector", "sd", ldq[b])
                p.raw("vector", lambda e, i=XQB[b], thi=thi, cs=cs: e.tensor_scalar(
                    out=HB[:, 0:cs], in0=i, scalar1=THC[:, thi:thi + 1],
                    scalar2=None, op0=ALU.is_ge))
                rdq[b] = p.inc("vector", "sv", lambda e, i=XQB[b], thi=thi, cs=cs: e.tensor_scalar(
                    out=HB[:, 4096:4096 + cs], in0=i, scalar1=THC[:, thi + 1:thi + 2],
                    scalar2=None, op0=ALU.is_le))
                p.wait("vector", "sd")
                sc = p.inc("vector", "sv", lambda e, cs=cs: e.tensor_tensor(
                    out=HB[:, 8192:8192 + cs], in0=HB[:, 0:cs], in1=HB[:, 4096:4096 + cs],
                    op=ALU.subtract))
                p.wait("sync", "sv", sc)
                p.dma(wdst[g * 128:(g + 1) * 128, j * cs:(j + 1) * cs], HB[:, 8192:8192 + cs])
        if DEBUG_B:
            p.dma(dbg_thc, THC[:])
            p.dma(dbg_wq1, wq1_s[0:128, :])
        p.barrier()

        def emit_ln(dst_bf):
            p.wait("tensor", "sv")
            p.wait("tensor", "sd")
            qwait(0)
            for k in range(KD):
                p.inc("tensor", "sp", lambda e, k=k: e.matmul(
                    Q0[0:1, :], ONES_COL_F, XF[:, k * C:(k + 1) * C],
                    start=(k == 0), stop=(k == KD - 1)))
            sqf = [("sv", 0), ("sv", 0)]
            qwait(1)
            for k in range(KD):
                sl = k % 2
                p.wait("scalar", sqf[sl][0], sqf[sl][1])
                p.wait("scalar", "sd")
                p.inc("scalar", "sa", lambda e, k=k, sl=sl: e.activation(
                    out=SQ[sl], in_=XF[:, k * C:(k + 1) * C], func=AF.Square))
                p.wait("tensor", "sa")
                cntp = p.inc("tensor", "sp", lambda e, k=k, sl=sl: e.matmul(
                    Q1[0:1, :], ONES_COL_B, SQ[sl], start=(k == 0), stop=(k == KD - 1)))
                sqf[sl] = ("sp", cntp)
            p.wait("vector", "sp")
            p.raw("vector", lambda e: e.tensor_scalar(out=RM[:], in0=Q0[0:1, :], scalar1=1.0 / D, scalar2=None, op0=ALU.mult))
            p.inc("vector", "sv", lambda e: e.tensor_scalar(out=RE2[:], in0=Q1[0:1, :], scalar1=1.0 / D, scalar2=None, op0=ALU.mult))
            p.wait("vector", "sv")
            p.inc("vector", "sv", lambda e: e.tensor_tensor(out=RV[:], in0=RM[:], in1=RM[:], op=ALU.mult))
            p.wait("vector", "sv")
            p.inc("vector", "sv", lambda e: e.tensor_tensor(out=RV[:], in0=RE2[:], in1=RV[:], op=ALU.subtract))
            qrel(0, "sv"); qrel(1, "sv")
            p.wait("scalar", "sv")
            p.inc("scalar", "sa", lambda e: e.activation(out=RS[:], in_=RV[:], func=AF.Ln, bias=1e-5))
            p.wait("scalar", "sa")
            p.inc("scalar", "sa", lambda e: e.activation(out=RS[:], in_=RS[:], func=AF.Exp, scale=-0.5))
            p.wait("tensor", "sa")
            qwait(2); qwait(3)
            p.inc("tensor", "sp", lambda e: e.matmul(Q2[:, :], ONES_ROW_F, RM[0:1, :], start=True, stop=True))
            p.inc("tensor", "sp", lambda e: e.matmul(Q3[:, :], ONES_ROW_F, RS[0:1, :], start=True, stop=True))
            p.wait("vector", "sp")
            for k in range(KD):
                p.raw("vector", lambda e, k=k: e.tensor_tensor(
                    out=SCR[:], in0=XF[:, k * C:(k + 1) * C], in1=Q2[:, :], op=ALU.subtract))
                p.inc("vector", "sv", lambda e, k=k, dst=dst_bf: e.tensor_tensor(
                    out=dst[:, k * C:(k + 1) * C], in0=SCR[:], in1=Q3[:, :], op=ALU.mult))
            qrel(2, "sv"); qrel(3, "sv")

        for b in range(NB):
            p.inc("vector", "sv", lambda e: e.memset(GV[:], 0))
            for tc in range(4):
                t0 = tc * C
                p.wait("sync", "sv")
                p.wait("sync", "sa")
                p.wait("sync", "sp")
                p.dma(XF.rearrange("p (g f) -> p g f", g=KD),
                      x1T[b].rearrange("(g p) t -> p g t", p=128)[:, :, t0:t0 + C])
                emit_ln(XNB)
                for k in range(KD):
                    p.inc("vector", "sv", lambda e, k=k: e.tensor_reduce(
                        out=ACC[:, 1:2], in_=XNB[:, k * C:(k + 1) * C], axis=mybir.AxisListType.X, op=ALU.add))
                    p.wait("vector", "sv")
                    p.inc("vector", "sv", lambda e, k=k: e.tensor_tensor(
                        out=GV[:, k:k + 1], in0=GV[:, k:k + 1], in1=ACC[:, 1:2], op=ALU.add))
                    p.wait("vector", "sv")
            p.inc("vector", "sv", lambda e: e.tensor_scalar(out=GV[:], in0=GV[:], scalar1=1.0 / T, scalar2=None, op0=ALU.mult))
            p.wait("vector", "sv")
            p.inc("vector", "sv", lambda e: e.tensor_copy(GVB[:], GV[:]))
            p.barrier(); qreset()
            for pt in range(4):
                for kg in range(KD):
                    p.inc("tensor", "sp", lambda e, pt=pt, kg=kg: e.matmul(
                        Q0[:, 0:1], GC1[:, kg * 512 + pt * 128:kg * 512 + (pt + 1) * 128],
                        GVB[:, kg:kg + 1], start=(kg == 0), stop=(kg == KD - 1)))
                p.wait("scalar", "sp")
                p.inc("scalar", "sa", lambda e, pt=pt: e.activation(
                    out=H1G[:, pt:pt + 1], in_=Q0[:, 0:1], func=AF.Gelu, bias=gcb1_c(pt)))
                p.wait("tensor", "sa")
            for pt in range(2):
                for kg in range(4):
                    p.inc("tensor", "sp", lambda e, pt=pt, kg=kg: e.matmul(
                        Q0[:, 0:1], GC2[:, kg * 256 + pt * 128:kg * 256 + (pt + 1) * 128],
                        H1G[:, kg:kg + 1], start=(kg == 0), stop=(kg == 3)))
                p.wait("scalar", "sp")
                p.inc("scalar", "sa", lambda e, pt=pt: e.activation(
                    out=H2G[:, pt:pt + 1], in_=Q0[:, 0:1], func=AF.Gelu, bias=gcb2_c(pt)))
                p.wait("tensor", "sa")
            for kg in range(2):
                p.inc("tensor", "sp", lambda e, kg=kg: e.matmul(
                    Q0[0:1, 0:1], GC3[:, kg:kg + 1], H2G[:, kg:kg + 1], start=(kg == 0), stop=(kg == 1)))
            p.wait("scalar", "sp")
            p.inc("scalar", "sa", lambda e: e.activation(
                out=RSM[0:1, 5:6], in_=Q0[0:1, 0:1], func=AF.Tanh, scale=0.5, bias=RSM[0:1, 2:3]))
            p.wait("vector", "sa")
            p.inc("vector", "sv", lambda e: e.tensor_scalar(out=RSM[0:1, 6:7], in0=RSM[0:1, 5:6], scalar1=0.5, scalar2=0.5, op0=ALU.mult, op1=ALU.add))
            p.wait("vector", "sv")
            p.inc("vector", "sv", lambda e: e.tensor_scalar(out=RSM[0:1, 7:8], in0=RSM[0:1, 6:7], scalar1=1.4, scalar2=-1.0, op0=ALU.mult, op1=ALU.add))
            p.wait("vector", "sv")
            p.inc("vector", "sv", lambda e: e.tensor_scalar(out=RSM[0:1, 7:8], in0=RSM[0:1, 7:8], scalar1=CROW[0:1, 0:1], scalar2=0.1, op0=ALU.mult, op1=ALU.add))
            p.wait("vector", "sv")
            p.inc("vector", "sv", lambda e: e.tensor_scalar(out=RSM[0:1, 8:9], in0=CROW[0:1, 0:1], scalar1=0.3, scalar2=None, op0=ALU.mult))
            p.wait("vector", "sv")
            p.inc("vector", "sv", lambda e: e.scalar_tensor_tensor(
                out=RSM[0:1, 9:10], in0=RSM[0:1, 8:9], scalar=1.0, in1=RSM[0:1, 7:8], op0=ALU.mult, op1=ALU.add))
            if DEBUG_B and b == 0:
                p.wait("sync", "sv")
                p.wait("sync", "sa")
                p.dma(dbg_rsm, RSM[:])
            p.barrier(); qreset()

            for tc in range(4):
                t0 = tc * C
                p.wait("sync", "sv")
                p.wait("sync", "sa")
                p.wait("sync", "sp")
                p.wait("sync", "sd")
                p.dma(XF.rearrange("p (g f) -> p g f", g=KD),
                      x1T[b].rearrange("(g p) t -> p g t", p=128)[:, :, t0:t0 + C])
                p.dma(REFC[:], refm[:, t0:t0 + C])
                emit_ln(OA)
                p.barrier(); qreset()
                for cyc in range(3):
                    SRC = OA if cyc % 2 == 0 else OB
                    DST = OB if cyc % 2 == 0 else OA
                    for pt in range(KD):
                        qi = pt % 4
                        qwait(qi)
                        for kg in range(KD):
                            p.inc("tensor", "sp", lambda e, pt=pt, kg=kg, qi=qi, SRC=SRC: e.matmul(
                                QS[qi][:, :], CN1[:, kg * 1024 + pt * 128:kg * 1024 + (pt + 1) * 128],
                                SRC[:, kg * C:(kg + 1) * C], start=(kg == 0), stop=(kg == KD - 1)))
                        p.wait("scalar", "sp")
                        p.inc("scalar", "sa", lambda e, pt=pt, qi=qi: e.activation(
                            out=H1B[:, pt * C:(pt + 1) * C], in_=QS[qi][:, :], func=AF.Gelu, bias=cnb1_c(pt)))
                        qrel(qi, "sa")
                    p.wait("tensor", "sa")
                    for pt in range(4):
                        qi = pt % 4
                        qwait(qi)
                        for kg in range(KD):
                            p.inc("tensor", "sp", lambda e, pt=pt, kg=kg, qi=qi: e.matmul(
                                QS[qi][:, :], CN2[:, kg * 512 + pt * 128:kg * 512 + (pt + 1) * 128],
                                H1B[:, kg * C:(kg + 1) * C], start=(kg == 0), stop=(kg == KD - 1)))
                        p.wait("scalar", "sp")
                        p.inc("scalar", "sa", lambda e, pt=pt, qi=qi: e.activation(
                            out=H2B[:, pt * C:(pt + 1) * C], in_=QS[qi][:, :], func=AF.Gelu, bias=cnb2_c(pt)))
                        qrel(qi, "sa")
                    p.wait("tensor", "sa")
                    for pt in range(2):
                        qi = pt % 2
                        qwait(qi)
                        for kg in range(4):
                            p.inc("tensor", "sp", lambda e, pt=pt, kg=kg, qi=qi: e.matmul(
                                QS[qi][:, :], CN3[:, kg * 256 + pt * 128:kg * 256 + (pt + 1) * 128],
                                H2B[:, kg * C:(kg + 1) * C], start=(kg == 0), stop=(kg == 3)))
                        p.wait("scalar", "sp")
                        p.inc("scalar", "sa", lambda e, pt=pt, qi=qi: e.activation(
                            out=H3B[:, pt * C:(pt + 1) * C], in_=QS[qi][:, :], func=AF.Gelu, bias=cnb3_c(pt)))
                        qrel(qi, "sa")
                    p.wait("tensor", "sa")
                    qwait(2)
                    for kg in range(2):
                        p.inc("tensor", "sp", lambda e, kg=kg: e.matmul(
                            Q2[0:1, :], CN4[:, kg:kg + 1], H3B[:, kg * C:(kg + 1) * C],
                            start=(kg == 0), stop=(kg == 1)))
                    p.wait("scalar", "sp")
                    p.inc("scalar", "sa", lambda e: e.activation(
                        out=TFR[:], in_=Q2[0:1, :], func=AF.Tanh, scale=0.5, bias=RSM[0:1, 1:2]))
                    qrel(2, "sa")
                    p.wait("vector", "sa")
                    if DEBUG_B and b == 0 and tc == 0 and cyc == 0:
                        p.barrier()
                        p.dma(dbg_tfr, TFR[:])
                        p.dma(dbg_h1, H1B[:])
                        p.dma(dbg_h2, H2B[:])
                        p.dma(dbg_h3, H3B[:])
                        p.dma(dbg_oa, OA[:])
                        p.barrier()
                    p.inc("vector", "sv", lambda e: e.tensor_scalar(
                        out=CPR[:], in0=TFR[:], scalar1=RSM[0:1, 8:9], scalar2=RSM[0:1, 9:10],
                        op0=ALU.mult, op1=ALU.add))
                    p.wait("tensor", "sv")
                    qwait(3)
                    p.inc("tensor", "sp", lambda e: e.matmul(Q3[:, :], ONES_ROW_F, CPR[0:1, :], start=True, stop=True))
                    p.wait("vector", "sp")
                    for k in range(KD):
                        src_t, sgn = TSGN[k]
                        p.raw("vector", lambda e, k=k, src_t=src_t, sgn=sgn, SRC=SRC: e.scalar_tensor_tensor(
                            out=MSK1[:], in0=SRC[:, src_t * C:(src_t + 1) * C], scalar=sgn,
                            in1=Q3[:, :], op0=ALU.mult, op1=ALU.mult))
                        p.inc("vector", "sv", lambda e, k=k, SRC=SRC, DST=DST: e.tensor_tensor(
                            out=DST[:, k * C:(k + 1) * C], in0=SRC[:, k * C:(k + 1) * C],
                            in1=MSK1[:], op=ALU.add))
                    qrel(3, "sv")
                    p.barrier(); qreset()
                for k in range(KD):
                    p.inc("vector", "sv", lambda e, k=k: e.tensor_tensor(
                        out=XF[:, k * C:(k + 1) * C], in0=XF[:, k * C:(k + 1) * C],
                        in1=OB[:, k * C:(k + 1) * C], op=ALU.add))
                p.barrier()
                if DEBUG_B and b == 0:
                    p.dma(dbg_x2.rearrange("(g p) t -> p g t", p=128)[:, :, t0:t0 + C],
                          XF.rearrange("p (g f) -> p g f", g=KD))
                    p.barrier()
                emit_ln(XNB)
                p.barrier(); qreset()

                if tc == 0:
                    p.inc("vector", "sv", lambda e: e.memset(VTD[:], 0))
                    for pt in range(KD):
                        buf = pt % 2
                        p.wait("sync", "sp")
                        p.dma(WS[:, buf * 4096:buf * 4096 + 1024].rearrange("p (g f) -> p g f", g=KD),
                              wv_s.rearrange("(g p) e -> p g e", p=128)[:, :, pt * 128:(pt + 1) * 128])
                        qi = pt % 2
                        qwait(qi)
                        p.wait("tensor", "sd")
                        p.wait("tensor", "sv")
                        for kg in range(KD):
                            p.inc("tensor", "sp", lambda e, kg=kg, buf=buf, qi=qi: e.matmul(
                                QS[qi][:, 0:14], WS[:, buf * 4096 + kg * 128:buf * 4096 + (kg + 1) * 128],
                                XNB[:, kg * C:kg * C + 14], start=(kg == 0), stop=(kg == KD - 1)))
                        p.wait("scalar", "sp")
                        p.inc("scalar", "sa", lambda e, pt=pt, qi=qi: e.activation(
                            out=VAL[:, pt * 14:(pt + 1) * 14], in_=QS[qi][:, 0:14], func=AF.Identity, bias=bv_c(pt)))
                        qrel(qi, "sa")
                    p.barrier(); qreset()
                    for pt in range(KD):
                        p.wait("tensor", "sv")
                        p.inc("tensor", "sp", lambda e, pt=pt: e.matmul(
                            P2[0:14, 0:128], VAL[:, pt * 14:(pt + 1) * 14], EYE[:, :],
                            start=True, stop=True))
                        p.wait("vector", "sp")
                        rb = 64 * (pt % 2)
                        p.raw("vector", lambda e, pt=pt, rb=rb: e.tensor_copy(
                            VTD[rb:rb + 14, pt * 128:pt * 128 + 64], P2[0:14, 0:64]))
                        p.inc("vector", "sv", lambda e, pt=pt, rb=rb: e.tensor_copy(
                            VTD[rb + 32:rb + 46, pt * 128 + 64:pt * 128 + 128], P2[0:14, 64:128]))
                    p.barrier(); qreset()
                for (wres, dst_fn) in ((WOX, 0), (WOY, 1), (WAT, 2)):
                    qi = dst_fn
                    qwait(qi)
                    for kg in range(KD):
                        p.inc("tensor", "sp", lambda e, kg=kg, wres=wres, qi=qi: e.matmul(
                            QS[qi][:, :], wres[:, kg * 128:(kg + 1) * 128],
                            XNB[:, kg * C:(kg + 1) * C], start=(kg == 0), stop=(kg == KD - 1)))
                p.wait("vector", "sp")
                p.raw("vector", lambda e: e.scalar_tensor_tensor(
                    out=IX[:], in0=Q0[:, :], scalar=boffx_c, in1=REFC[:], op0=ALU.add, op1=ALU.add))
                p.raw("vector", lambda e: e.tensor_scalar(
                    out=IY[:], in0=Q1[:, :], scalar1=boffy_c, scalar2=-0.5, op0=ALU.add, op1=ALU.add))
                p.inc("vector", "sv", lambda e: e.tensor_scalar(
                    out=AWL[:], in0=Q2[:, :], scalar1=batt_c, scalar2=None, op0=ALU.add))
                qrel(0, "sv"); qrel(1, "sv"); qrel(2, "sv")
                p.wait("tensor", "sv")
                qwait(3)
                p.inc("tensor", "sp", lambda e: e.matmul(Q3[:, :], SAVG[:, :], AWL[:], start=True, stop=True))
                p.wait("vector", "sp")
                p.inc("vector", "sv", lambda e: e.tensor_tensor(out=M4[:], in0=AWL[:], in1=Q3[:, :], op=ALU.subtract))
                qrel(3, "sv")
                p.wait("scalar", "sv")
                p.inc("scalar", "sa", lambda e: e.activation(out=EB[:], in_=M4[:], func=AF.Exp))
                p.wait("tensor", "sa")
                qwait(4)
                p.inc("tensor", "sp", lambda e: e.matmul(Q4[:, :], SSUM[:, :], EB[:], start=True, stop=True))
                p.wait("vector", "sp")
                p.raw("vector", lambda e: e.reciprocal(M4[:], Q4[:, :]))
                p.raw("vector", lambda e: e.tensor_tensor(out=AWL[:], in0=EB[:], in1=M4[:], op=ALU.mult))
                qrel(4, "sv")
                p.inc("vector", "sv", lambda e: e.engine_nop() if False else e.tensor_copy(RSM[0:1, 15:16], RSM[0:1, 15:16]))
                p.wait("scalar", "sv")
                p.inc("scalar", "sa", lambda e: e.activation(out=HAT[:], in_=IX[:], func=AF.Abs))
                p.wait("vector", "sa")
                p.raw("vector", lambda e: e.tensor_scalar(out=HAT[:], in0=HAT[:], scalar1=1.0, scalar2=-1.0, op0=ALU.min, op1=ALU.mult))
                p.inc("vector", "sv", lambda e: e.scalar_tensor_tensor(
                    out=GB[:], in0=HAT[:], scalar=1.0, in1=AWL[:], op0=ALU.add, op1=ALU.mult))
                p.barrier(); qreset()
                for j in range(14):
                    p.wait("scalar", "sv")
                    p.wait("scalar", "sp")
                    bias_j = 0.0 if j == 0 else -float(j)
                    p.inc("scalar", "sa", lambda e, bias_j=bias_j: e.activation(
                        out=HAT[:], in_=IY[:], func=AF.Abs, bias=bias_j))
                    p.wait("vector", "sa")
                    p.raw("vector", lambda e: e.scalar_tensor_tensor(
                        out=MSK1[:], in0=HAT[:], scalar=1.0, in1=GB[:], op0=ALU.min, op1=ALU.mult))
                    p.inc("vector", "sv", lambda e: e.tensor_tensor(
                        out=MSK2[:], in0=GB[:], in1=MSK1[:], op=ALU.subtract))
                    p.wait("tensor", "sv")
                    for grp in range(4):
                        p.inc("tensor", "sp", lambda e, j=j, grp=grp: e.matmul(
                            QS[grp][:, :], SELC[:, grp * 141 + 13 - j:grp * 141 + 13 - j + 128],
                            MSK2[:], start=(j == 0), stop=(j == 13)))
                p.wait("vector", "sp")
                for grp in range(4):
                    p.inc("vector", "sv", lambda e, grp=grp: e.tensor_copy(
                        AJ[:, grp * C:(grp + 1) * C], QS[grp][:, :]))
                p.barrier(); qreset()
                for pt in range(KD):
                    qi = pt % 2
                    qwait(qi)
                    rb = 64 * (pt % 2)
                    p.inc("tensor", "sp", lambda e, pt=pt, qi=qi, rb=rb: e.matmul(
                        QS[qi][:, :], VTD[rb:rb + 64, pt * 128:(pt + 1) * 128],
                        AJ[rb:rb + 64, (pt // 2) * C:(pt // 2 + 1) * C],
                        start=True, stop=True))
                    p.wait("scalar", "sp")
                    p.inc("scalar", "sa", lambda e, pt=pt, qi=qi: e.copy(
                        out=ATN[:, pt * C:(pt + 1) * C], in_=QS[qi][:, :]))
                    qrel(qi, "sa")
                p.barrier(); qreset()
                wfree = [("sv", 0), ("sv", 0)]
                for pt in range(KD):
                    buf = pt % 2
                    p.wait("sync", wfree[buf][0], wfree[buf][1])
                    p.dma(WS[:, buf * 4096:buf * 4096 + 1024].rearrange("p (g f) -> p g f", g=KD),
                          wout_s.rearrange("(g p) e -> p g e", p=128)[:, :, pt * 128:(pt + 1) * 128])
                    qi = pt % 2
                    qwait(qi)
                    p.wait("tensor", "sd")
                    for kg in range(KD):
                        p.inc("tensor", "sp", lambda e, kg=kg, buf=buf, qi=qi: e.matmul(
                            QS[qi][:, :], WS[:, buf * 4096 + kg * 128:buf * 4096 + (kg + 1) * 128],
                            ATN[:, kg * C:(kg + 1) * C], start=(kg == 0), stop=(kg == KD - 1)))
                    wfree[buf] = ("sp", p.cnt["sp"])
                    p.wait("scalar", "sp")
                    p.inc("scalar", "sa", lambda e, pt=pt, qi=qi: e.activation(
                        out=O1B[:, pt * C:(pt + 1) * C], in_=QS[qi][:, :], func=AF.Identity, bias=bout_c(pt)))
                    qrel(qi, "sa")
                p.barrier(); qreset()
                wfree = [("sv", 0), ("sv", 0)]
                for pt in range(KD):
                    buf = pt % 2
                    p.wait("sync", wfree[buf][0], wfree[buf][1])
                    p.dma(WS[:, buf * 4096:buf * 4096 + 1024].rearrange("p (g f) -> p g f", g=KD),
                          wproj_s.rearrange("(g p) e -> p g e", p=128)[:, :, pt * 128:(pt + 1) * 128])
                    qi = pt % 2
                    qwait(qi)
                    p.wait("tensor", "sd")
                    for kg in range(KD):
                        p.inc("tensor", "sp", lambda e, kg=kg, buf=buf, qi=qi: e.matmul(
                            QS[qi][:, :], WS[:, buf * 4096 + kg * 128:buf * 4096 + (kg + 1) * 128],
                            O1B[:, kg * C:(kg + 1) * C], start=(kg == 0), stop=(kg == KD - 1)))
                    wfree[buf] = ("sp", p.cnt["sp"])
                    p.wait("vector", "sp")
                    p.inc("vector", "sv", lambda e, pt=pt, qi=qi: e.scalar_tensor_tensor(
                        out=XF[:, pt * C:(pt + 1) * C], in0=QS[qi][:, :], scalar=bproj_c(pt),
                        in1=XF[:, pt * C:(pt + 1) * C], op0=ALU.add, op1=ALU.add))
                    qrel(qi, "sv")
                p.barrier(); qreset()
                if DEBUG_B and b == 0:
                    p.dma(dbg_x3.rearrange("(g p) t -> p g t", p=128)[:, :, t0:t0 + C],
                          XF.rearrange("p (g f) -> p g f", g=KD))
                    p.dma(dbg_aw, AWL[:])
                    p.dma(dbg_aj, AJ[:])
                    p.dma(dbg_vtd, VTD[:])
                    p.dma(dbg_gq, GB[:])
                    p.barrier()
                emit_ln(XNB)
                p.barrier(); qreset()
                wslots = [WS[:, i * 1024:(i + 1) * 1024] for i in range(8)]
                sfree = [("sv", 0)] * 8
                for et in range(32):
                    sl = et % 8
                    p.wait("sync", sfree[sl][0], sfree[sl][1])
                    p.dma(wslots[sl].rearrange("p (g f) -> p g f", g=KD),
                          wq1_s.rearrange("(g p) e -> p g e", p=128)[:, :, et * 128:(et + 1) * 128])
                    qi = et % 4
                    qwait(qi)
                    p.wait("tensor", "sd")
                    for kg in range(KD):
                        p.inc("tensor", "sp", lambda e, kg=kg, sl=sl, qi=qi: e.matmul(
                            QS[qi][:, :], wslots[sl][:, kg * 128:(kg + 1) * 128],
                            XNB[:, kg * C:(kg + 1) * C], start=(kg == 0), stop=(kg == KD - 1)))
                    sfree[sl] = ("sp", p.cnt["sp"])
                    p.wait("scalar", "sp")
                    p.inc("scalar", "sa", lambda e, et=et, qi=qi: e.activation(
                        out=HB[:, et * C:(et + 1) * C], in_=QS[qi][:, :], func=AF.Relu, bias=b1_c(et)))
                    qrel(qi, "sa")
                p.barrier(); qreset()
                if DEBUG_B and b == 0 and tc == 0:
                    p.dma(dbg_hh1, HB[:])
                    p.barrier()
                wfree = [("sv", 0), ("sv", 0)]
                for pt in range(KD):
                    buf = pt % 2
                    p.wait("sync", wfree[buf][0], wfree[buf][1])
                    p.dma(WS[:, buf * 4096:(buf + 1) * 4096].rearrange("p (g f) -> p g f", g=32),
                          wq2_s.rearrange("(g p) e -> p g e", p=128)[:, :, pt * 128:(pt + 1) * 128])
                    qi = pt % 2
                    qwait(qi)
                    p.wait("tensor", "sd")
                    for kg in range(32):
                        p.inc("tensor", "sp", lambda e, kg=kg, buf=buf, qi=qi: e.matmul(
                            QS[qi][:, :], WS[:, buf * 4096 + kg * 128:buf * 4096 + (kg + 1) * 128],
                            HB[:, kg * C:(kg + 1) * C], start=(kg == 0), stop=(kg == 31)))
                    wfree[buf] = ("sp", p.cnt["sp"])
                    p.wait("vector", "sp")
                    p.inc("vector", "sv", lambda e, pt=pt, qi=qi: e.scalar_tensor_tensor(
                        out=XF[:, pt * C:(pt + 1) * C], in0=QS[qi][:, :], scalar=b2_c(pt),
                        in1=XF[:, pt * C:(pt + 1) * C], op0=ALU.add, op1=ALU.add))
                    qrel(qi, "sv")
                    p.wait("sync", "sv")
                    p.dma(yT[b][pt * 128:(pt + 1) * 128, t0:t0 + C], XF[:, pt * C:(pt + 1) * C])
                p.barrier(); qreset()
        p.wait("sync", "sd")
        p.build_block()
        es.close()
    return nc


def prep_phase_b_inputs(inputs, x1):
    g = lambda k: np.asarray(inputs[k])
    NH, NP = 16, 4
    colb = np.zeros((128, 88), np.float32)
    colb[:, 0:8] = _tile_cols(g("L1_attn_b_value"), KD)
    colb[:, 8:16] = _tile_cols(g("L1_attn_b_out"), KD)
    colb[:, 16:24] = _tile_cols(g("L1_attn_b_proj"), KD)
    colb[:, 24:32] = _tile_cols(g("L0_ffn_cn_b1"), KD)
    colb[:, 32:36] = _tile_cols(g("L0_ffn_cn_b2"), 4)
    colb[:, 36:38] = _tile_cols(g("L0_ffn_cn_b3"), 2)
    colb[:, 38:42] = _tile_cols(g("L0_ffn_gc_b1"), 4)
    colb[:, 42:44] = _tile_cols(g("L0_ffn_gc_b2"), 2)
    batt = g("L1_attn_b_attn")
    boff = g("L1_attn_b_off")
    for pp in range(NP):
        for hh in range(NH):
            colb[32 * pp + hh, 44] = batt[hh * 4 + pp]
            colb[32 * pp + hh, 45] = boff[hh * 8 + pp * 2 + 0]
            colb[32 * pp + hh, 46] = boff[hh * 8 + pp * 2 + 1]
    colb[:, 48:80] = _tile_cols(g("L1_ffn_b1"), 32)
    colb[:, 80:88] = _tile_cols(g("L1_ffn_b2"), KD)
    crows = np.zeros((1, 8), np.float32)
    crows[0, 0] = np.asarray(inputs["L0_ffn_adaptive_range"])[0]
    crows[0, 1] = np.asarray(inputs["L0_ffn_cn_b4"])[0]
    crows[0, 2] = np.asarray(inputs["L0_ffn_gc_b3"])[0]
    ref = (np.linspace(0.0, 1.0, T, dtype=np.float32) - 0.5).astype(np.float32)
    refm = np.ascontiguousarray(np.broadcast_to(ref[None, :], (128, T)))
    woffx = np.zeros((D, 128), np.float32)
    woffy = np.zeros((D, 128), np.float32)
    wattn = np.zeros((D, 128), np.float32)
    woff_full = g("L1_attn_W_off")
    wattn_full = g("L1_attn_W_attn")
    for pp in range(NP):
        for hh in range(NH):
            woffx[:, 32 * pp + hh] = woff_full[:, hh * 8 + pp * 2 + 0]
            woffy[:, 32 * pp + hh] = woff_full[:, hh * 8 + pp * 2 + 1]
            wattn[:, 32 * pp + hh] = wattn_full[:, hh * 4 + pp]
    selc = np.zeros((128, 4 * 141), dtype=BF)
    for gidx in range(4):
        for hl in range(4):
            hh = 4 * gidx + hl
            for pp in range(4):
                selc[32 * pp + hh, gidx * 141 + 32 * hl + 13] = 1
    eye128 = np.eye(128, dtype=BF)
    savg = np.zeros((128, 128), np.float32)
    ssumb = np.zeros((128, 128), dtype=BF)
    for hh in range(32):
        for pp in range(4):
            for pp2 in range(4):
                savg[32 * pp + hh, 32 * pp2 + hh] = 0.25
                ssumb[32 * pp + hh, 32 * pp2 + hh] = 1
    consts_b = np.zeros((128, 130), dtype=BF)
    consts_b[:, 0] = 1
    consts_b[0, 1:129] = 1
    consts_f = np.zeros((128, 152), np.float32)
    consts_f[:, 0] = 1
    consts_f[0, 1:129] = 1
    for _i, _v in enumerate(ACT_CONSTS):
        consts_f[:, 129 + _i] = _v

    shared = dict(
        cn_w1=g("L0_ffn_cn_w1"), cn_w2=g("L0_ffn_cn_w2"), cn_w3=g("L0_ffn_cn_w3"),
        cn_w4=g("L0_ffn_cn_w4"), gc_w1=g("L0_ffn_gc_w1"), gc_w2=g("L0_ffn_gc_w2"),
        gc_w3=g("L0_ffn_gc_w3"), wv=g("L1_attn_W_value"), wout=g("L1_attn_W_out"),
        wproj=g("L1_attn_W_proj"), woffx=woffx, woffy=woffy, wattn=wattn,
        w1T=np.ascontiguousarray(g("L1_ffn_W1").T), w2T=np.ascontiguousarray(g("L1_ffn_W2").T),
        colb=colb, crows=crows, refm=refm, selc=selc, eye128=eye128, savg=savg, ssumb=ssumb,
        consts_b=consts_b, consts_f=consts_f)
    maps = []
    for c_ in range(8):
        m = dict(shared)
        m["x1T"] = np.ascontiguousarray(x1[c_ * NB:(c_ + 1) * NB].transpose(0, 2, 1))
        maps.append(m)
    return maps


def run_phase_b(inputs, x1):
    nc = build_phase_b()
    maps = prep_phase_b_inputs(inputs, x1)
    res = run_bass_kernel_spmd(nc, maps, core_ids=list(range(8)))
    y = np.empty((16, T, D), np.float32)
    for c_ in range(8):
        y[c_ * NB:(c_ + 1) * NB] = res.results[c_]["yT"].transpose(0, 2, 1)
    return y



def build_fused():
    nc = bass.Bass("TRN2", target_bir_lowering=False, debug=False)
    xT = nc.dram_tensor("xT", [NB, D, T], F32, kind="ExternalInput").ap()
    winT = nc.dram_tensor("winT", [D, 2 * DI], F32, kind="ExternalInput").ap()
    xprojT = nc.dram_tensor("xprojT", [DI, 128], F32, kind="ExternalInput").ap()
    dtprojT = nc.dram_tensor("dtprojT", [DTR, DI], F32, kind="ExternalInput").ap()
    outprojT = nc.dram_tensor("outprojT", [DI, D], F32, kind="ExternalInput").ap()
    alog_t = nc.dram_tensor("alog_t", [128, KE * NST], F32, kind="ExternalInput").ap()
    cols = nc.dram_tensor("cols", [128, KE * 4], F32, kind="ExternalInput").ap()
    convw = nc.dram_tensor("convw", [128, KE * 4], F32, kind="ExternalInput").ap()
    consts_b = nc.dram_tensor("consts_b", [128, 130], BF16, kind="ExternalInput").ap()
    consts_f = nc.dram_tensor("consts_f", [128, 152], F32, kind="ExternalInput").ap()
    eye64 = nc.dram_tensor("eye64", [64, 64], BF16, kind="ExternalInput").ap()
    x1T = nc.dram_tensor("x1T", [NB, D, T], F32, kind="ExternalOutput").ap()
    win_s = nc.dram_tensor("win_s", [D, 2 * DI], BF16, kind="Internal").ap()
    rows_s = nc.dram_tensor("rows_s", [2, 2 * C], F32, kind="Internal").ap()
    opj_s = nc.dram_tensor("opj_s", [DI, D], BF16, kind="Internal").ap()

    from contextlib import ExitStack
    es = ExitStack()
    UCV = es.enter_context(nc.sbuf_tensor("UCV", [128, KE * H], BF16))
    DU = es.enter_context(nc.sbuf_tensor("DU", [128, KE * H], BF16))
    SZ = es.enter_context(nc.sbuf_tensor("SZ", [128, KE * H], BF16))
    DT = es.enter_context(nc.sbuf_tensor("DT", [128, KE * H], BF16))
    ARF = es.enter_context(nc.sbuf_tensor("ARF", [128, 4608], F32))
    ARB = es.enter_context(nc.sbuf_tensor("ARB", [128, 9216], BF16))
    YA = es.enter_context(nc.sbuf_tensor("YA", [128, 4 * H], F32))
    HST = es.enter_context(nc.sbuf_tensor("HST", [128, KE * NST], F32))
    XPJ = es.enter_context(nc.sbuf_tensor("XPJ", [128, KE * 128], BF16))
    DTPJ = es.enter_context(nc.sbuf_tensor("DTPJ", [64, DI], BF16))
    ASB = es.enter_context(nc.sbuf_tensor("ASB", [128, KE * NST], F32))
    COLS = es.enter_context(nc.sbuf_tensor("COLS", [128, KE * 4], F32))
    CW = es.enter_context(nc.sbuf_tensor("CW", [128, KE * 4], F32))
    CNB = es.enter_context(nc.sbuf_tensor("CNB", [128, 130], BF16))
    CNF = es.enter_context(nc.sbuf_tensor("CNF", [128, 152], F32))
    EYEB = es.enter_context(nc.sbuf_tensor("EYEB", [64, 64], BF16))
    DTL = es.enter_context(nc.sbuf_tensor("DTL", [64, H], BF16))
    BCB = es.enter_context(nc.sbuf_tensor("BCB", [64, H], BF16))
    CVH = es.enter_context(nc.sbuf_tensor("CVH", [128, KE * 3], BF16))
    RM = es.enter_context(nc.sbuf_tensor("RM", [1, C], F32))
    RE2 = es.enter_context(nc.sbuf_tensor("RE2", [1, C], F32))
    RV = es.enter_context(nc.sbuf_tensor("RV", [1, C], F32))
    RS = es.enter_context(nc.sbuf_tensor("RS", [1, C], F32))
    RR = es.enter_context(nc.sbuf_tensor("RR", [1, C], F32))

    PH0 = es.enter_context(nc.psum_tensor("PH0", [128, H], F32))
    PH1 = es.enter_context(nc.psum_tensor("PH1", [128, H], F32))
    PB = es.enter_context(nc.psum_tensor("PB", [128, H], F32))
    PC = es.enter_context(nc.psum_tensor("PC", [128, H], F32))
    P0 = PH0[:, 0:C]
    P1 = PH0[:, C:2 * C]
    P2 = PH1[:, 0:C]
    P3 = PH1[:, C:2 * C]
    sd = es.enter_context(nc.semaphore("sd"))
    sp = es.enter_context(nc.semaphore("sp"))
    sa = es.enter_context(nc.semaphore("sa"))
    sv = es.enter_context(nc.semaphore("sv"))
    sg = es.enter_context(nc.semaphore("sg"))
    if True:
        p = Prog(nc, {"sd": sd, "sp": sp, "sa": sa, "sv": sv, "sg": sg})
        PS = [P0, P1, P2, P3]

        XC = ARF[:, 0:4096]
        S1 = ARF[:, 4096:4608]
        S2 = ARF[:, 0:512]
        RT = S1[0:1, 0:C]
        DAv = [ARF[:, 0:1024], ARF[:, 1024:2048]]
        HHS = ARF[:, 2048:3072]
        CHS = ARF[:, 3072:4096]
        X1O = [ARF[:, 4096:4608], ARF[:, 4096:4608]]

        SQB = [ARB[:, 0:512], ARB[:, 512:1024]]
        XRB = ARB[:, 1024:5120]
        WIN = [ARB[:, 5120:7168], ARB[:, 7168:9216]]
        BBb = ARB[:, 0:1024]
        BBC = ARB[:, 1024:2048]
        CBC = ARB[:, 2048:3072]
        CVT = ARB[:, 3072:4100]
        WOP = [ARB[:, 5120:7168], ARB[:, 7168:9216]]

        ONES_COL_B = CNB[:, 0:1]
        ONES_ROW_F = CNF[0:1, 1:129]
        ONES_COL_F = CNF[:, 0:1]
        for _i, _v in enumerate(ACT_CONSTS):
            nc.const_aps.register(F32, float(_v), CNF[:, 129 + _i:130 + _i])

        p.dma(COLS[:], cols)
        p.dma(CW[:], convw)
        p.dma(CNB[:], consts_b)
        p.dma(CNF[:], consts_f)
        p.dma(EYEB[:], eye64)
        p.wait("scalar", "sd")
        p.wait("vector", "sd")
        p.wait("tensor", "sd")
        p.wait("gpsimd", "sd")
        p.dma(XC[:, 0:KE * 128].rearrange("p (g f) -> p g f", g=KE),
              xprojT.rearrange("(g p) f -> p g f", p=128))
        p.wait("vector", "sd")
        p.inc("vector", "sv", lambda e: e.tensor_copy(XPJ[:], XC[:, 0:KE * 128]))
        p.wait("sync", "sv")
        p.dma(XC[0:DTR, 0:DI], dtprojT)
        p.wait("vector", "sd")
        p.inc("vector", "sv", lambda e: e.tensor_copy(DTPJ[:], XC[0:DTR, 0:DI]))
        p.wait("sync", "sv")
        p.dma(XC[:, 0:KE * NST], alog_t)
        p.wait("scalar", "sd")
        p.raw("scalar", lambda e: e.activation(out=ASB[:], in_=XC[:, 0:KE * NST], func=AF.Exp))
        p.inc("scalar", "sa", lambda e: e.mul(ASB[:], ASB[:], -1.0))
        p.wait("sync", "sv")
        p.wait("sync", "sa")
        for (src, dst, nrows, ncols) in ((winT, win_s, D, 2 * DI), (outprojT, opj_s, DI, D)):
            for g in range(nrows // 128):
                p.wait("sync", "sd")
                p.dma(XC[:, 0:ncols], src[g * 128:(g + 1) * 128, :])
                p.wait("vector", "sd")
                p.inc("vector", "sv", lambda e, ncc=ncols: e.tensor_copy(ARB[:, 0:ncc], XC[:, 0:ncc]))
                p.wait("sync", "sv")
                p.dma(dst[g * 128:(g + 1) * 128, :], ARB[:, 0:ncols])
        p.barrier()

        for b in range(NB):
            p.raw("vector", lambda e: e.memset(CVH[:], 0))
            p.inc("vector", "sv", lambda e: e.memset(HST[:], 0))
            p.barrier()
            for hf in range(2):
                toff = hf * H
                for tc in range(2):
                    t0 = toff + tc * C
                    hc = tc * C
                    p.wait("sync", "sv")
                    p.wait("sync", "sp")
                    p.dma(XC.rearrange("p (g f) -> p g f", g=KD),
                          xT[b].rearrange("(g p) t -> p g t", p=128)[:, :, t0:t0 + C])
                    p.wait("tensor", "sd")
                    for k in range(KD):
                        p.inc("tensor", "sp", lambda e, k=k: e.matmul(
                            P0[0:1, :], ONES_COL_F, XC[:, k * C:(k + 1) * C],
                            start=(k == 0), stop=(k == KD - 1)))
                    sq_free = [0, 0]
                    p.wait("scalar", "sd")
                    for k in range(KD):
                        sl = k % 2
                        p.wait("scalar", "sp", sq_free[sl])
                        p.inc("scalar", "sa", lambda e, k=k, sl=sl: e.activation(
                            out=SQB[sl], in_=XC[:, k * C:(k + 1) * C], func=AF.Square))
                        p.wait("tensor", "sa")
                        sq_free[sl] = p.inc("tensor", "sp", lambda e, k=k, sl=sl: e.matmul(
                            P1[0:1, :], ONES_COL_B, SQB[sl],
                            start=(k == 0), stop=(k == KD - 1)))
                    p.wait("vector", "sp")
                    p.raw("vector", lambda e: e.tensor_scalar(out=RM[:], in0=P0[0:1, :], scalar1=1.0 / D, scalar2=None, op0=ALU.mult))
                    p.inc("vector", "sv", lambda e: e.tensor_scalar(out=RE2[:], in0=P1[0:1, :], scalar1=1.0 / D, scalar2=None, op0=ALU.mult))
                    p.wait("vector", "sv")
                    p.inc("vector", "sv", lambda e: e.tensor_tensor(out=RT[:], in0=RM[:], in1=RM[:], op=ALU.mult))
                    p.wait("vector", "sv")
                    p.inc("vector", "sv", lambda e: e.tensor_tensor(out=RV[:], in0=RE2[:], in1=RT[:], op=ALU.subtract))
                    p.wait("scalar", "sv")
                    p.inc("scalar", "sa", lambda e: e.activation(out=RT[:], in_=RV[:], func=AF.Ln, bias=1e-5))
                    p.wait("scalar", "sa")
                    p.inc("scalar", "sa", lambda e: e.activation(out=RS[:], in_=RT[:], func=AF.Exp, scale=-0.5))
                    p.wait("sync", "sa")
                    p.dma(rows_s[0:1, hc:hc + C], RM[:])
                    p.dma(rows_s[1:2, hc:hc + C], RS[:])
                    p.wait("vector", "sa")
                    p.inc("vector", "sv", lambda e: e.tensor_tensor(out=RT[:], in0=RS[:], in1=RS[:], op=ALU.mult))
                    p.wait("vector", "sv")
                    p.inc("vector", "sv", lambda e: e.tensor_tensor(out=RT[:], in0=RV[:], in1=RT[:], op=ALU.mult))
                    p.wait("scalar", "sv")
                    p.inc("scalar", "sa", lambda e: e.activation(out=RT[:], in_=RT[:], func=AF.Ln, bias=1e-6))
                    p.wait("scalar", "sa")
                    p.inc("scalar", "sa", lambda e: e.activation(out=RT[:], in_=RT[:], func=AF.Exp, scale=-0.5))
                    p.wait("vector", "sa")
                    p.inc("vector", "sv", lambda e: e.tensor_tensor(out=RR[:], in0=RS[:], in1=RT[:], op=ALU.mult))
                    p.wait("tensor", "sv")
                    p.inc("tensor", "sp", lambda e: e.matmul(P2[:, :], ONES_ROW_F, RM[0:1, :], start=True, stop=True))
                    p.inc("tensor", "sp", lambda e: e.matmul(P3[:, :], ONES_ROW_F, RR[0:1, :], start=True, stop=True))
                    p.wait("vector", "sp")
                    for k in range(KD):
                        p.raw("vector", lambda e, k=k: e.tensor_tensor(
                            out=S1[:, 0:C], in0=XC[:, k * C:(k + 1) * C], in1=P2[:, :], op=ALU.subtract))
                        p.inc("vector", "sv", lambda e, k=k: e.scalar_tensor_tensor(
                            out=XRB[:, k * C:(k + 1) * C], in0=S1[:, 0:C],
                            scalar=_col(COLS, k * 4 + 3), in1=P3[:, :], op0=ALU.mult, op1=ALU.mult))
                    xrb_ready = p.cnt["sv"]

                    win_used = [0, 0]
                    win_ld = [0, 0]
                    ps_free = [0, 0, 0, 0]
                    ps_sem = ["sv", "sv", "sv", "sv"]
                    for eb in range(16):
                        buf = eb % 2
                        if eb == 0:
                            p.wait("sync", "sp", win_used[buf])
                            win_ld[0] = p.dma(WIN[0].rearrange("p (g f) -> p g f", g=KD),
                                              win_s.rearrange("(g p) e -> p g e", p=128)[:, :, 0:256])
                        if eb + 1 < 16:
                            nbuf = (eb + 1) % 2
                            p.wait("sync", "sp", win_used[nbuf])
                            win_ld[nbuf] = p.dma(WIN[nbuf].rearrange("p (g f) -> p g f", g=KD),
                                                 win_s.rearrange("(g p) e -> p g e", p=128)[:, :, (eb + 1) * 256:(eb + 2) * 256])
                        p.wait("tensor", "sd", win_ld[buf])
                        p.wait("tensor", "sv", xrb_ready)
                        for e2 in range(2):
                            eg = eb * 2 + e2
                            ps_i = eg % 4
                            p.wait("tensor", ps_sem[ps_i], ps_free[ps_i])
                            for k in range(KD):
                                p.inc("tensor", "sp", lambda e, buf=buf, k=k, e2=e2, ps_i=ps_i: e.matmul(
                                    PS[ps_i][:, :], WIN[buf][:, k * 256 + e2 * 128:k * 256 + (e2 + 1) * 128],
                                    XRB[:, k * C:(k + 1) * C], start=(k == 0), stop=(k == KD - 1)))
                            win_used[buf] = p.cnt["sp"]
                            p.wait("scalar", "sp")
                            if eg < KE:
                                dstc = eg * H + hc
                                p.inc("scalar", "sa", lambda e, ps_i=ps_i, dstc=dstc: e.copy(
                                    out=UCV[:, dstc:dstc + C], in_=PS[ps_i][:, :]))
                            else:
                                dstc = (eg - KE) * H + hc
                                p.inc("scalar", "sa", lambda e, ps_i=ps_i, dstc=dstc: e.activation(
                                    out=SZ[:, dstc:dstc + C], in_=PS[ps_i][:, :], func=AF.Silu))
                            ps_free[ps_i] = p.cnt["sa"]
                            ps_sem[ps_i] = "sa"
                    p.barrier()
                for k in range(KE):
                    p.raw("vector", lambda e, k=k: e.tensor_copy(CVT[:, 0:3], CVH[:, k * 3:(k + 1) * 3]))
                    p.raw("vector", lambda e, k=k: e.tensor_copy(CVT[:, 3:H + 3], UCV[:, k * H:(k + 1) * H]))
                    p.raw("vector", lambda e, k=k: e.tensor_copy(CVH[:, k * 3:(k + 1) * 3], CVT[:, H:H + 3]))
                    p.raw("vector", lambda e, k=k: e.tensor_scalar(
                        out=CHS[:], in0=CVT[:, 0:H], scalar1=_col(CW, k * 4 + 0), scalar2=None, op0=ALU.mult))
                    for j in range(1, 4):
                        fn = lambda e, k=k, j=j: e.scalar_tensor_tensor(
                            out=CHS[:], in0=CVT[:, j:j + H], scalar=_col(CW, k * 4 + j), in1=CHS[:],
                            op0=ALU.mult, op1=ALU.add)
                        if j == 3:
                            p.inc("vector", "sv", fn)
                        else:
                            p.raw("vector", fn)
                    p.wait("scalar", "sv")
                    p.inc("scalar", "sa", lambda e, k=k: e.activation(
                        out=UCV[:, k * H:(k + 1) * H], in_=CHS[:], func=AF.Silu, bias=_col(COLS, k * 4 + 0)))
                    p.wait("vector", "sa")
                p.barrier()
                dt_ps_free = {1: 0, 2: 0}
                for tc in range(2):
                    hc = tc * C
                    p.wait("tensor", "sv")
                    for k in range(KE):
                        p.inc("tensor", "sp", lambda e, k=k, hc=hc: e.matmul(
                            P0[:, :], XPJ[:, k * 128:(k + 1) * 128],
                            UCV[:, k * H + hc:k * H + hc + C], start=(k == 0), stop=(k == KE - 1)))
                    p.wait("vector", "sp")
                    p.raw("vector", lambda e, hc=hc: e.tensor_copy(DTL[0:64, hc:hc + C], P0[0:64, :]))
                    p.raw("vector", lambda e, hc=hc: e.tensor_copy(BCB[0:32, hc:hc + C], P0[64:96, :]))
                    p.inc("vector", "sv", lambda e, hc=hc: e.tensor_copy(BCB[32:64, hc:hc + C], P0[96:128, :]))
                    p.wait("tensor", "sv")
                    for k in range(KE):
                        ps_i = 1 + (k % 2)
                        p.wait("tensor", "sa", dt_ps_free[ps_i])
                        p.inc("tensor", "sp", lambda e, k=k, hc=hc, ps_i=ps_i: e.matmul(
                            PS[ps_i][:, :], DTPJ[0:64, k * 128:(k + 1) * 128],
                            DTL[0:64, hc:hc + C], start=True, stop=True))
                        p.wait("scalar", "sp")
                        p.wait("scalar", "sv")
                        p.raw("scalar", lambda e, k=k, ps_i=ps_i: e.activation(
                            out=S1[:, 0:C], in_=PS[ps_i][:, :], func=AF.Abs, bias=_col(COLS, k * 4 + 1)))
                        p.raw("scalar", lambda e: e.activation(out=S2[:, 0:C], in_=S1[:, 0:C], func=AF.Exp, scale=-1.0))
                        p.raw("scalar", lambda e: e.activation(out=S1[:, 0:C], in_=S2[:, 0:C], func=AF.Ln, bias=1.0))
                        p.inc("scalar", "sa", lambda e, k=k, ps_i=ps_i: e.activation(
                            out=S2[:, 0:C], in_=PS[ps_i][:, :], func=AF.Relu, bias=_col(COLS, k * 4 + 1)))
                        dt_ps_free[ps_i] = p.cnt["sa"]
                        p.wait("vector", "sa")
                        p.inc("vector", "sv", lambda e, k=k, hc=hc: e.tensor_tensor(
                            out=DT[:, k * H + hc:k * H + hc + C], in0=S1[:, 0:C], in1=S2[:, 0:C], op=ALU.add))
                p.barrier()
                for k in range(KE):
                    p.inc("vector", "sv", lambda e, k=k: e.tensor_tensor(
                        out=DU[:, k * H:(k + 1) * H], in0=DT[:, k * H:(k + 1) * H],
                        in1=UCV[:, k * H:(k + 1) * H], op=ALU.mult))
                p.barrier()
                SC_DA = [ARF[:, 0:1024], ARF[:, 1024:2048]]
                SC_HHP = [ARF[:, 2048:3072], ARF[:, 3072:4096]]
                SC_HHV = [PH0[:, :], PH1[:, :]]
                SC_BB = ARB[:, 0:1024]
                SC_BBC = [ARB[:, 1024:2048], ARB[:, 3072:4096]]
                SC_CBC = [ARB[:, 2048:3072], ARB[:, 4096:5120]]
                SC_CHp = ARB[:, 5120:6144]
                da_free = [("sv", 0)] * 2
                hhp_free = [("sg", 0), ("sg", 0)]
                bbc_free = [("sv", 0, "sg", 0)] * 2
                bbc_ready = [0, 0]
                pbpc_box = [0]
                hst_box = [0]
                it_ctr = [0]

                def emit_bcast(n_):
                    buf = n_ % 2
                    sem0, c0, sem1, c1 = bbc_free[buf]
                    p.wait("tensor", sem0, c0)
                    p.wait("tensor", sem1, c1)
                    p.wait("tensor", "sa", pbpc_box[0])
                    for cc in range(2):
                        p.inc("tensor", "sp", lambda e, n_=n_, cc=cc: e.matmul(
                            PB[:, cc * C:(cc + 1) * C], EYEB[:, n_:n_ + 1].broadcast_to([64, 128]),
                            BCB[0:64, cc * C:(cc + 1) * C], start=True, stop=True))
                        p.inc("tensor", "sp", lambda e, n_=n_, cc=cc: e.matmul(
                            PC[:, cc * C:(cc + 1) * C], EYEB[:, 32 + n_:33 + n_].broadcast_to([64, 128]),
                            BCB[0:64, cc * C:(cc + 1) * C], start=True, stop=True))
                    p.wait("scalar", "sp")
                    p.raw("scalar", lambda e, buf=buf: e.copy(out=SC_BBC[buf], in_=PB[:, :]))
                    p.inc("scalar", "sa", lambda e, buf=buf: e.copy(out=SC_CBC[buf], in_=PC[:, :]))
                    bbc_ready[buf] = p.cnt["sa"]
                    pbpc_box[0] = p.cnt["sa"]

                for dh in range(4):
                    if dh == 0:
                        emit_bcast(0)
                    for n in range(NST):
                        buf = n % 2
                        if n + 1 < NST:
                            emit_bcast(n + 1)
                        elif dh + 1 < 4:
                            emit_bcast(0)
                        for k8 in range(4):
                            k = dh * 4 + k8
                            par = it_ctr[0] % 2
                            it_ctr[0] += 1
                            sem_d, cnt_d = da_free[par]
                            p.wait("scalar", sem_d, cnt_d)
                            ec = p.inc("scalar", "sa", lambda e, k=k, n=n, par=par: e.activation(
                                out=SC_DA[par], in_=DT[:, k * H:(k + 1) * H], func=AF.Exp,
                                scale=ASB[:, k * NST + n:k * NST + n + 1]))
                            init = 0.0 if hf == 0 else HST[:, k * NST + n:k * NST + n + 1]
                            p.wait("vector", "sa", bbc_ready[buf])
                            p.raw("vector", lambda e, k=k, buf=buf: e.tensor_tensor(
                                out=SC_BB[:], in0=DU[:, k * H:(k + 1) * H], in1=SC_BBC[buf], op=ALU.mult))
                            p.wait("vector", "sa", ec)
                            p.wait("vector", "sa", hst_box[0])
                            if k8 < 2:
                                HH = SC_HHV[k8]
                                sc_cnt = p.inc("vector", "sv", lambda e, par=par, HH=HH, init=init: e.tensor_tensor_scan(
                                    out=HH, data0=SC_DA[par], data1=SC_BB[:], initial=init,
                                    op0=ALU.mult, op1=ALU.add))
                                p.raw("vector", lambda e, par=par, HH=HH, buf=buf: e.tensor_tensor(
                                    out=SC_DA[par], in0=HH, in1=SC_CBC[buf], op=ALU.mult))
                                if n == 0:
                                    cv = p.inc("vector", "sv", lambda e, k8=k8, par=par: e.tensor_copy(
                                        YA[:, k8 * H:(k8 + 1) * H], SC_DA[par]))
                                else:
                                    cv = p.inc("vector", "sv", lambda e, k8=k8, par=par: e.tensor_tensor(
                                        out=YA[:, k8 * H:(k8 + 1) * H], in0=YA[:, k8 * H:(k8 + 1) * H],
                                        in1=SC_DA[par], op=ALU.add))
                                da_free[par] = ("sv", cv)
                            else:
                                HH = SC_HHP[k8 - 2]
                                sem_h, cnt_h = hhp_free[k8 - 2]
                                p.wait("vector", sem_h, cnt_h)
                                sc_cnt = p.inc("vector", "sv", lambda e, par=par, HH=HH, init=init: e.tensor_tensor_scan(
                                    out=HH, data0=SC_DA[par], data1=SC_BB[:], initial=init,
                                    op0=ALU.mult, op1=ALU.add))
                                da_free[par] = ("sv", sc_cnt)
                                p.wait("gpsimd", "sv", sc_cnt)
                                p.raw("gpsimd", lambda e, HH=HH, buf=buf: e.tensor_tensor(
                                    out=SC_CHp[:], in0=HH, in1=SC_CBC[buf], op=ALU.mult))
                                if n == 0:
                                    cg = p.inc("gpsimd", "sg", lambda e, k8=k8: e.tensor_copy(
                                        YA[:, k8 * H:(k8 + 1) * H], SC_CHp[:]))
                                else:
                                    cg = p.inc("gpsimd", "sg", lambda e, k8=k8: e.tensor_tensor(
                                        out=YA[:, k8 * H:(k8 + 1) * H], in0=YA[:, k8 * H:(k8 + 1) * H],
                                        in1=SC_CHp[:], op=ALU.add))
                                hhp_free[k8 - 2] = ("sg", cg)
                            if hf == 0:
                                p.wait("scalar", "sv", sc_cnt)
                                hst_box[0] = p.inc("scalar", "sa", lambda e, k=k, n=n, HH=HH: e.copy(
                                    out=HST[:, k * NST + n:k * NST + n + 1], in_=HH[:, H - 1:H]))
                        bbc_free[buf] = ("sv", p.cnt["sv"], "sg", p.cnt["sg"])
                    p.wait("vector", "sg")
                    for k8 in range(4):
                        k = dh * 4 + k8
                        p.raw("vector", lambda e, k=k, k8=k8: e.scalar_tensor_tensor(
                            out=SC_DA[0], in0=UCV[:, k * H:(k + 1) * H], scalar=_col(COLS, k * 4 + 2),
                            in1=YA[:, k8 * H:(k8 + 1) * H], op0=ALU.mult, op1=ALU.add))
                        p.inc("vector", "sv", lambda e, k=k: e.tensor_tensor(
                            out=UCV[:, k * H:(k + 1) * H], in0=SC_DA[0], in1=SZ[:, k * H:(k + 1) * H],
                            op=ALU.mult))
                p.barrier()
                wop_used = [0, 0]
                ps_free2 = [0, 0]
                XCS = [ARF[:, 0:C], ARF[:, C:2 * C]]
                XNS = ARF[:, 2 * C:3 * C]
                X1S = [ARF[:, 3 * C:4 * C], ARF[:, 4 * C:5 * C]]
                for tc in range(2):
                    t0 = toff + tc * C
                    hc = tc * C
                    p.dma(RM[:], rows_s[0:1, hc:hc + C])
                    p.dma(RS[:], rows_s[1:2, hc:hc + C])
                    p.wait("tensor", "sv")
                    p.wait("tensor", "sd")
                    p.inc("tensor", "sp", lambda e: e.matmul(
                        PB[:, 0:C], ONES_ROW_F, RM[0:1, :], start=True, stop=True))
                    p.inc("tensor", "sp", lambda e: e.matmul(
                        PC[:, 0:C], ONES_ROW_F, RS[0:1, :], start=True, stop=True))
                    wop_ld = [0, 0]
                    for dm in range(KD):
                        buf = dm % 2
                        p.wait("sync", "sv")
                        p.dma(XCS[buf], xT[b][dm * 128:(dm + 1) * 128, t0:t0 + C])
                        if dm == 0:
                            p.wait("sync", "sp", wop_used[0])
                            wop_ld[0] = p.dma(WOP[0].rearrange("p (g f) -> p g f", g=KE),
                                              opj_s.rearrange("(g p) d -> p g d", p=128)[:, :, 0:128])
                        if dm + 1 < KD:
                            nbuf = (dm + 1) % 2
                            p.wait("sync", "sp", wop_used[nbuf])
                            wop_ld[nbuf] = p.dma(WOP[nbuf].rearrange("p (g f) -> p g f", g=KE),
                                                 opj_s.rearrange("(g p) d -> p g d", p=128)[:, :, (dm + 1) * 128:(dm + 2) * 128])
                        ps_i = dm % 2
                        p.wait("tensor", "sv", ps_free2[ps_i])
                        p.wait("tensor", "sd", wop_ld[buf])
                        for k in range(KE):
                            p.inc("tensor", "sp", lambda e, k=k, buf=buf, hc=hc, ps_i=ps_i: e.matmul(
                                PS[ps_i][:, :], WOP[buf][:, k * 128:(k + 1) * 128],
                                UCV[:, k * H + hc:k * H + hc + C], start=(k == 0), stop=(k == KE - 1)))
                        wop_used[buf] = p.cnt["sp"]
                        p.wait("vector", "sp")
                        p.wait("vector", "sd")
                        ob = dm % 2
                        p.raw("vector", lambda e, buf=buf: e.tensor_tensor(
                            out=XNS, in0=XCS[buf], in1=PB[:, 0:C], op=ALU.subtract))
                        p.raw("vector", lambda e: e.tensor_tensor(
                            out=XNS, in0=XNS, in1=PC[:, 0:C], op=ALU.mult))
                        p.raw("vector", lambda e, ps_i=ps_i, ob=ob, buf=buf: e.tensor_tensor(
                            out=X1S[ob], in0=PS[ps_i][:, :], in1=XCS[buf], op=ALU.add))
                        p.inc("vector", "sv", lambda e, ob=ob: e.tensor_tensor(
                            out=X1S[ob], in0=X1S[ob], in1=XNS, op=ALU.add))
                        ps_free2[ps_i] = p.cnt["sv"]
                        p.wait("sync", "sv")
                        p.dma(x1T[b][dm * 128:(dm + 1) * 128, t0:t0 + C], X1S[ob])
                p.barrier()

        p.barrier()
        XF = ARF[:, 0:4096]
        OA = UCV[:, 0:4096]
        OB = UCV[:, 4096:8192]
        XNB = UCV[:, 8192:12288]
        HB = DU[:, 0:16384]
        WS = SZ[:, 0:8192]
        CN1 = SZ[:, 8192:16384]
        CN2 = DT[:, 0:4096]
        CN3 = DT[:, 4096:5120]
        GC1 = DT[:, 5120:9216]
        GC2 = DT[:, 9216:10240]
        WOX = DT[:, 10240:11264]
        WOY = DT[:, 11264:12288]
        WAT = DT[:, 12288:13312]
        SELC = DT[:, 13312:13876]
        CN4 = DT[:, 13876:13878]
        GC3 = DT[:, 13878:13880]
        VAL = ARB[:, 0:112]
        VTD = ARB[:, 112:1136]
        GB = ARB[:, 1136:1648]
        MSK1 = ARB[:, 1648:2160]
        MSK2 = ARB[:, 2160:2672]
        AJ = ARB[:, 2672:4720]
        EB = ARB[:, 4720:5232]
        AWL = YA[:, 0:512]
        M4 = YA[:, 512:1024]
        IX = YA[:, 1024:1536]
        IY = YA[:, 1536:2048]
        HAT = YA[:, 2048:2560]
        REFC = YA[:, 2560:3072]
        SCR = S1
        Q0 = P0[:, :]
        Q1 = P1[:, :]
        Q2 = P2[:, :]
        Q3 = P3[:, :]
        Q4 = PB[:, 0:C]
        Q5 = PB[:, 512:1024]
        XPJF = XPJ.bitcast(F32)
        COLB = XPJF[:, 0:88]
        CROW = XPJF[0:1, 88:96]
        GV = XPJF[:, 96:104]
        THC = XPJF[:, 104:108]
        ACC = XPJF[:, 108:110]
        RSM = XPJF[0:1, 110:126]
        SAVG = XPJF[:, 128:256]
        GVB = XPJ[:, 512:520]
        H1G = XPJ[:, 520:524]
        H2G = XPJ[:, 524:526]
        SSUM = XPJ[:, 528:656]
        EYE = XPJ[:, 656:784]
        TFR = RR
        CPR = BCB[0:64, :].bitcast(F32)[0:1, 0:C]
        cn_w1 = nc.dram_tensor("cn_w1", [D, D], F32, kind="ExternalInput").ap()
        cn_w2 = nc.dram_tensor("cn_w2", [D, 512], F32, kind="ExternalInput").ap()
        cn_w3 = nc.dram_tensor("cn_w3", [512, 256], F32, kind="ExternalInput").ap()
        cn_w4 = nc.dram_tensor("cn_w4", [256, 1], F32, kind="ExternalInput").ap()
        gc_w1 = nc.dram_tensor("gc_w1", [D, 512], F32, kind="ExternalInput").ap()
        gc_w2 = nc.dram_tensor("gc_w2", [512, 256], F32, kind="ExternalInput").ap()
        gc_w3 = nc.dram_tensor("gc_w3", [256, 1], F32, kind="ExternalInput").ap()
        wv = nc.dram_tensor("wv", [D, D], F32, kind="ExternalInput").ap()
        wout = nc.dram_tensor("wout", [D, D], F32, kind="ExternalInput").ap()
        wproj = nc.dram_tensor("wproj", [D, D], F32, kind="ExternalInput").ap()
        woffx = nc.dram_tensor("woffx", [D, 128], F32, kind="ExternalInput").ap()
        woffy = nc.dram_tensor("woffy", [D, 128], F32, kind="ExternalInput").ap()
        wattn = nc.dram_tensor("wattn", [D, 128], F32, kind="ExternalInput").ap()
        w1T = nc.dram_tensor("w1T", [D, 4 * D], F32, kind="ExternalInput").ap()
        w2T = nc.dram_tensor("w2T", [4 * D, D], F32, kind="ExternalInput").ap()
        colb = nc.dram_tensor("colb", [128, 88], F32, kind="ExternalInput").ap()
        crows = nc.dram_tensor("crows", [1, 8], F32, kind="ExternalInput").ap()
        refm = nc.dram_tensor("refm", [128, T], F32, kind="ExternalInput").ap()
        selc = nc.dram_tensor("selc", [128, 4 * 141], BF16, kind="ExternalInput").ap()
        eye128 = nc.dram_tensor("eye128", [128, 128], BF16, kind="ExternalInput").ap()
        savg = nc.dram_tensor("savg", [128, 128], F32, kind="ExternalInput").ap()
        ssumb = nc.dram_tensor("ssumb", [128, 128], BF16, kind="ExternalInput").ap()
        yT = nc.dram_tensor("yT", [NB, D, T], F32, kind="ExternalOutput").ap()
        wq1_s = nc.dram_tensor("wq1_s", [D, 4 * D], BF16, kind="Internal").ap()
        wq2_s = nc.dram_tensor("wq2_s", [4 * D, D], BF16, kind="Internal").ap()
        wv_s = nc.dram_tensor("wv_s", [D, D], BF16, kind="Internal").ap()
        wout_s = nc.dram_tensor("wout_s", [D, D], BF16, kind="Internal").ap()
        wproj_s = nc.dram_tensor("wproj_s", [D, D], BF16, kind="Internal").ap()

        QS = [Q0, Q1, Q2, Q3, Q4, Q5]
        qfree = {}

        def qwait(i):
            sem, cnt = qfree.get(i, ("sv", 0))
            p.wait("tensor", sem, cnt)

        def qrel(i, sem):
            qfree[i] = (sem, p.cnt[sem])

        def qreset():
            qfree.clear()

        H1B = HB[:, 0:4096]
        H2B = HB[:, 4096:6144]
        H3B = HB[:, 6144:7168]
        SQ = [HB[:, 7168:7680], HB[:, 7680:8192]]
        ATN = OA
        O1B = OB
        WSa = WS[:, 0:4096]
        WSb = WS[:, 4096:8192]

        ONES_COL_B = CNB[:, 0:1]
        ONES_ROW_F = CNF[0:1, 1:129]
        ONES_COL_F = CNF[:, 0:1]

        bv_c = lambda pt: _col(COLB, 0 + pt)
        bout_c = lambda pt: _col(COLB, 8 + pt)
        bproj_c = lambda pt: _col(COLB, 16 + pt)
        cnb1_c = lambda pt: _col(COLB, 24 + pt)
        cnb2_c = lambda pt: _col(COLB, 32 + pt)
        cnb3_c = lambda pt: _col(COLB, 36 + pt)
        gcb1_c = lambda pt: _col(COLB, 38 + pt)
        gcb2_c = lambda pt: _col(COLB, 42 + pt)
        batt_c = _col(COLB, 44)
        boffx_c = _col(COLB, 45)
        boffy_c = _col(COLB, 46)
        b1_c = lambda et: _col(COLB, 48 + et)
        b2_c = lambda pt: _col(COLB, 80 + pt)

        p.dma(COLB[:], colb)
        p.dma(CROW[:], crows)
        p.dma(SELC[:], selc)
        p.dma(EYE[:], eye128)
        p.dma(SAVG[:], savg)
        p.dma(SSUM[:], ssumb)
        for e_ in Prog.ENGINES:
            p.wait(e_, "sd")

        def load_w(srcap, dst, nrows, ncols, dst_stride):
            for g in range(nrows // 128):
                p.wait("sync", "sv")
                p.dma(XF[:, 0:ncols], srcap[g * 128:(g + 1) * 128, :])
                p.wait("vector", "sd")
                p.inc("vector", "sv", lambda e, g=g, ncols=ncols, dst=dst, ds=dst_stride: e.tensor_copy(
                    dst[:, g * ds:g * ds + ncols], XF[:, 0:ncols]))

        load_w(cn_w1, CN1, D, 1024, 1024)
        load_w(cn_w2, CN2, D, 512, 512)
        load_w(cn_w3, CN3, 512, 256, 256)
        load_w(cn_w4, CN4, 256, 1, 1)
        load_w(gc_w1, GC1, D, 512, 512)
        load_w(gc_w2, GC2, 512, 256, 256)
        load_w(gc_w3, GC3, 256, 1, 1)
        for (asrc, adst) in ((wv, wv_s), (wout, wout_s), (wproj, wproj_s)):
            for g in range(KD):
                p.wait("sync", "sv")
                p.wait("sync", "sd")
                p.dma(XF[:, 0:1024], asrc[g * 128:(g + 1) * 128, :])
                p.wait("vector", "sd")
                p.inc("vector", "sv", lambda e: e.tensor_copy(HB[:, 0:1024], XF[:, 0:1024]))
                p.wait("sync", "sv")
                p.dma(adst[g * 128:(g + 1) * 128, :], HB[:, 0:1024])
        load_w(woffx, WOX, D, 128, 128)
        load_w(woffy, WOY, D, 128, 128)
        load_w(wattn, WAT, D, 128, 128)
        p.raw("vector", lambda e: e.tensor_scalar(out=RSM[0:1, 1:2], in0=CROW[0:1, 1:2], scalar1=0.5, scalar2=None, op0=ALU.mult))
        p.inc("vector", "sv", lambda e: e.tensor_scalar(out=RSM[0:1, 2:3], in0=CROW[0:1, 2:3], scalar1=0.5, scalar2=None, op0=ALU.mult))
        p.barrier()

        for (wsrc, wdst, ngr, ncols, thi) in ((w1T, wq1_s, 8, 4096, 0), (w2T, wq2_s, 32, 1024, 2)):
            cs = min(ncols, 2048)
            qtiles = [(g, j) for g in range(ngr) for j in range(ncols // cs)]
            XQB = [XF[:, 0:cs], XF[:, cs:2 * cs]]
            ldq = [0, 0]
            rdq = [0, 0]
            p.wait("sync", "sv")
            p.wait("sync", "sd")
            p.inc("vector", "sv", lambda e: e.memset(ACC[:, 0:1], 0))
            p.wait("vector", "sv")
            qg0, qj0 = qtiles[0]
            ldq[0] = p.dma(XQB[0], wsrc[qg0 * 128:(qg0 + 1) * 128, qj0 * cs:(qj0 + 1) * cs])
            for idx, (g, j) in enumerate(qtiles):
                b = idx % 2
                if idx + 1 < len(qtiles):
                    nb = (idx + 1) % 2
                    ng, nj = qtiles[idx + 1]
                    p.wait("sync", "sv", rdq[nb])
                    ldq[nb] = p.dma(XQB[nb], wsrc[ng * 128:(ng + 1) * 128, nj * cs:(nj + 1) * cs])
                p.wait("vector", "sd", ldq[b])
                rdq[b] = p.inc("vector", "sv", lambda e, i=XQB[b]: e.tensor_reduce(
                    out=ACC[:, 1:2], in_=i, axis=mybir.AxisListType.X,
                    op=ALU.add, apply_absolute_value=True))
                p.wait("vector", "sv")
                p.inc("vector", "sv", lambda e: e.tensor_tensor(
                    out=ACC[:, 0:1], in0=ACC[:, 0:1], in1=ACC[:, 1:2], op=ALU.add))
                p.wait("vector", "sv")
            p.wait("tensor", "sv")
            p.inc("tensor", "sp", lambda e: e.matmul(Q0[0:1, 0:1], ONES_COL_F, ACC[:, 0:1], start=True, stop=True))
            p.wait("vector", "sp")
            nrm = 1.0 / (ngr * 128 * ncols)
            p.inc("vector", "sv", lambda e, nrm=nrm: e.tensor_scalar(
                out=RSM[0:1, 4:5], in0=Q0[0:1, 0:1], scalar1=0.5 * nrm, scalar2=0.5e-5,
                op0=ALU.mult, op1=ALU.add))
            p.wait("tensor", "sv")
            p.inc("tensor", "sp", lambda e: e.matmul(Q1[:, 0:1], ONES_ROW_F, RSM[0:1, 4:5], start=True, stop=True))
            p.wait("vector", "sp")
            p.raw("vector", lambda e, thi=thi: e.tensor_copy(THC[:, thi:thi + 1], Q1[:, 0:1]))
            p.inc("vector", "sv", lambda e, thi=thi: e.tensor_scalar(
                out=THC[:, thi + 1:thi + 2], in0=Q1[:, 0:1], scalar1=-1.0, scalar2=None, op0=ALU.mult))
            p.wait("vector", "sv")
            p.wait("sync", "sv", rdq[0])
            ldq[0] = p.dma(XQB[0], wsrc[qg0 * 128:(qg0 + 1) * 128, qj0 * cs:(qj0 + 1) * cs])
            for idx, (g, j) in enumerate(qtiles):
                b = idx % 2
                if idx + 1 < len(qtiles):
                    nb = (idx + 1) % 2
                    ng, nj = qtiles[idx + 1]
                    p.wait("sync", "sv", rdq[nb])
                    ldq[nb] = p.dma(XQB[nb], wsrc[ng * 128:(ng + 1) * 128, nj * cs:(nj + 1) * cs])
                p.wait("vector", "sd", ldq[b])
                p.raw("vector", lambda e, i=XQB[b], thi=thi, cs=cs: e.tensor_scalar(
                    out=HB[:, 0:cs], in0=i, scalar1=THC[:, thi:thi + 1],
                    scalar2=None, op0=ALU.is_ge))
                rdq[b] = p.inc("vector", "sv", lambda e, i=XQB[b], thi=thi, cs=cs: e.tensor_scalar(
                    out=HB[:, 4096:4096 + cs], in0=i, scalar1=THC[:, thi + 1:thi + 2],
                    scalar2=None, op0=ALU.is_le))
                p.wait("vector", "sd")
                sc = p.inc("vector", "sv", lambda e, cs=cs: e.tensor_tensor(
                    out=HB[:, 8192:8192 + cs], in0=HB[:, 0:cs], in1=HB[:, 4096:4096 + cs],
                    op=ALU.subtract))
                p.wait("sync", "sv", sc)
                p.dma(wdst[g * 128:(g + 1) * 128, j * cs:(j + 1) * cs], HB[:, 8192:8192 + cs])
        p.barrier()

        def emit_ln(dst_bf):
            p.wait("tensor", "sv")
            p.wait("tensor", "sd")
            qwait(0)
            for k in range(KD):
                p.inc("tensor", "sp", lambda e, k=k: e.matmul(
                    Q0[0:1, :], ONES_COL_F, XF[:, k * C:(k + 1) * C],
                    start=(k == 0), stop=(k == KD - 1)))
            sqf = [("sv", 0), ("sv", 0)]
            qwait(1)
            for k in range(KD):
                sl = k % 2
                p.wait("scalar", sqf[sl][0], sqf[sl][1])
                p.wait("scalar", "sd")
                p.inc("scalar", "sa", lambda e, k=k, sl=sl: e.activation(
                    out=SQ[sl], in_=XF[:, k * C:(k + 1) * C], func=AF.Square))
                p.wait("tensor", "sa")
                cntp = p.inc("tensor", "sp", lambda e, k=k, sl=sl: e.matmul(
                    Q1[0:1, :], ONES_COL_B, SQ[sl], start=(k == 0), stop=(k == KD - 1)))
                sqf[sl] = ("sp", cntp)
            p.wait("vector", "sp")
            p.raw("vector", lambda e: e.tensor_scalar(out=RM[:], in0=Q0[0:1, :], scalar1=1.0 / D, scalar2=None, op0=ALU.mult))
            p.inc("vector", "sv", lambda e: e.tensor_scalar(out=RE2[:], in0=Q1[0:1, :], scalar1=1.0 / D, scalar2=None, op0=ALU.mult))
            p.wait("vector", "sv")
            p.inc("vector", "sv", lambda e: e.tensor_tensor(out=RV[:], in0=RM[:], in1=RM[:], op=ALU.mult))
            p.wait("vector", "sv")
            p.inc("vector", "sv", lambda e: e.tensor_tensor(out=RV[:], in0=RE2[:], in1=RV[:], op=ALU.subtract))
            qrel(0, "sv"); qrel(1, "sv")
            p.wait("scalar", "sv")
            p.inc("scalar", "sa", lambda e: e.activation(out=RS[:], in_=RV[:], func=AF.Ln, bias=1e-5))
            p.wait("scalar", "sa")
            p.inc("scalar", "sa", lambda e: e.activation(out=RS[:], in_=RS[:], func=AF.Exp, scale=-0.5))
            p.wait("tensor", "sa")
            qwait(2); qwait(3)
            p.inc("tensor", "sp", lambda e: e.matmul(Q2[:, :], ONES_ROW_F, RM[0:1, :], start=True, stop=True))
            p.inc("tensor", "sp", lambda e: e.matmul(Q3[:, :], ONES_ROW_F, RS[0:1, :], start=True, stop=True))
            p.wait("vector", "sp")
            for k in range(KD):
                p.raw("vector", lambda e, k=k: e.tensor_tensor(
                    out=SCR[:], in0=XF[:, k * C:(k + 1) * C], in1=Q2[:, :], op=ALU.subtract))
                p.inc("vector", "sv", lambda e, k=k, dst=dst_bf: e.tensor_tensor(
                    out=dst[:, k * C:(k + 1) * C], in0=SCR[:], in1=Q3[:, :], op=ALU.mult))
            qrel(2, "sv"); qrel(3, "sv")

        for b in range(NB):
            p.inc("vector", "sv", lambda e: e.memset(GV[:], 0))
            for tc in range(4):
                t0 = tc * C
                p.wait("sync", "sv")
                p.wait("sync", "sa")
                p.wait("sync", "sp")
                p.dma(XF.rearrange("p (g f) -> p g f", g=KD),
                      x1T[b].rearrange("(g p) t -> p g t", p=128)[:, :, t0:t0 + C])
                emit_ln(XNB)
                for k in range(KD):
                    p.inc("vector", "sv", lambda e, k=k: e.tensor_reduce(
                        out=ACC[:, 1:2], in_=XNB[:, k * C:(k + 1) * C], axis=mybir.AxisListType.X, op=ALU.add))
                    p.wait("vector", "sv")
                    p.inc("vector", "sv", lambda e, k=k: e.tensor_tensor(
                        out=GV[:, k:k + 1], in0=GV[:, k:k + 1], in1=ACC[:, 1:2], op=ALU.add))
                    p.wait("vector", "sv")
            p.inc("vector", "sv", lambda e: e.tensor_scalar(out=GV[:], in0=GV[:], scalar1=1.0 / T, scalar2=None, op0=ALU.mult))
            p.wait("vector", "sv")
            p.inc("vector", "sv", lambda e: e.tensor_copy(GVB[:], GV[:]))
            p.barrier(); qreset()
            for pt in range(4):
                for kg in range(KD):
                    p.inc("tensor", "sp", lambda e, pt=pt, kg=kg: e.matmul(
                        Q0[:, 0:1], GC1[:, kg * 512 + pt * 128:kg * 512 + (pt + 1) * 128],
                        GVB[:, kg:kg + 1], start=(kg == 0), stop=(kg == KD - 1)))
                p.wait("scalar", "sp")
                p.inc("scalar", "sa", lambda e, pt=pt: e.activation(
                    out=H1G[:, pt:pt + 1], in_=Q0[:, 0:1], func=AF.Gelu, bias=gcb1_c(pt)))
                p.wait("tensor", "sa")
            for pt in range(2):
                for kg in range(4):
                    p.inc("tensor", "sp", lambda e, pt=pt, kg=kg: e.matmul(
                        Q0[:, 0:1], GC2[:, kg * 256 + pt * 128:kg * 256 + (pt + 1) * 128],
                        H1G[:, kg:kg + 1], start=(kg == 0), stop=(kg == 3)))
                p.wait("scalar", "sp")
                p.inc("scalar", "sa", lambda e, pt=pt: e.activation(
                    out=H2G[:, pt:pt + 1], in_=Q0[:, 0:1], func=AF.Gelu, bias=gcb2_c(pt)))
                p.wait("tensor", "sa")
            for kg in range(2):
                p.inc("tensor", "sp", lambda e, kg=kg: e.matmul(
                    Q0[0:1, 0:1], GC3[:, kg:kg + 1], H2G[:, kg:kg + 1], start=(kg == 0), stop=(kg == 1)))
            p.wait("scalar", "sp")
            p.inc("scalar", "sa", lambda e: e.activation(
                out=RSM[0:1, 5:6], in_=Q0[0:1, 0:1], func=AF.Tanh, scale=0.5, bias=RSM[0:1, 2:3]))
            p.wait("vector", "sa")
            p.inc("vector", "sv", lambda e: e.tensor_scalar(out=RSM[0:1, 6:7], in0=RSM[0:1, 5:6], scalar1=0.5, scalar2=0.5, op0=ALU.mult, op1=ALU.add))
            p.wait("vector", "sv")
            p.inc("vector", "sv", lambda e: e.tensor_scalar(out=RSM[0:1, 7:8], in0=RSM[0:1, 6:7], scalar1=1.4, scalar2=-1.0, op0=ALU.mult, op1=ALU.add))
            p.wait("vector", "sv")
            p.inc("vector", "sv", lambda e: e.tensor_scalar(out=RSM[0:1, 7:8], in0=RSM[0:1, 7:8], scalar1=CROW[0:1, 0:1], scalar2=0.1, op0=ALU.mult, op1=ALU.add))
            p.wait("vector", "sv")
            p.inc("vector", "sv", lambda e: e.tensor_scalar(out=RSM[0:1, 8:9], in0=CROW[0:1, 0:1], scalar1=0.3, scalar2=None, op0=ALU.mult))
            p.wait("vector", "sv")
            p.inc("vector", "sv", lambda e: e.scalar_tensor_tensor(
                out=RSM[0:1, 9:10], in0=RSM[0:1, 8:9], scalar=1.0, in1=RSM[0:1, 7:8], op0=ALU.mult, op1=ALU.add))
            p.barrier(); qreset()

            for tc in range(4):
                t0 = tc * C
                p.wait("sync", "sv")
                p.wait("sync", "sa")
                p.wait("sync", "sp")
                p.wait("sync", "sd")
                p.dma(XF.rearrange("p (g f) -> p g f", g=KD),
                      x1T[b].rearrange("(g p) t -> p g t", p=128)[:, :, t0:t0 + C])
                p.dma(REFC[:], refm[:, t0:t0 + C])
                emit_ln(OA)
                p.barrier(); qreset()
                for cyc in range(3):
                    SRC = OA if cyc % 2 == 0 else OB
                    DST = OB if cyc % 2 == 0 else OA
                    for pt in range(KD):
                        qi = pt % 4
                        qwait(qi)
                        for kg in range(KD):
                            p.inc("tensor", "sp", lambda e, pt=pt, kg=kg, qi=qi, SRC=SRC: e.matmul(
                                QS[qi][:, :], CN1[:, kg * 1024 + pt * 128:kg * 1024 + (pt + 1) * 128],
                                SRC[:, kg * C:(kg + 1) * C], start=(kg == 0), stop=(kg == KD - 1)))
                        p.wait("scalar", "sp")
                        p.inc("scalar", "sa", lambda e, pt=pt, qi=qi: e.activation(
                            out=H1B[:, pt * C:(pt + 1) * C], in_=QS[qi][:, :], func=AF.Gelu, bias=cnb1_c(pt)))
                        qrel(qi, "sa")
                    p.wait("tensor", "sa")
                    for pt in range(4):
                        qi = pt % 4
                        qwait(qi)
                        for kg in range(KD):
                            p.inc("tensor", "sp", lambda e, pt=pt, kg=kg, qi=qi: e.matmul(
                                QS[qi][:, :], CN2[:, kg * 512 + pt * 128:kg * 512 + (pt + 1) * 128],
                                H1B[:, kg * C:(kg + 1) * C], start=(kg == 0), stop=(kg == KD - 1)))
                        p.wait("scalar", "sp")
                        p.inc("scalar", "sa", lambda e, pt=pt, qi=qi: e.activation(
                            out=H2B[:, pt * C:(pt + 1) * C], in_=QS[qi][:, :], func=AF.Gelu, bias=cnb2_c(pt)))
                        qrel(qi, "sa")
                    p.wait("tensor", "sa")
                    for pt in range(2):
                        qi = pt % 2
                        qwait(qi)
                        for kg in range(4):
                            p.inc("tensor", "sp", lambda e, pt=pt, kg=kg, qi=qi: e.matmul(
                                QS[qi][:, :], CN3[:, kg * 256 + pt * 128:kg * 256 + (pt + 1) * 128],
                                H2B[:, kg * C:(kg + 1) * C], start=(kg == 0), stop=(kg == 3)))
                        p.wait("scalar", "sp")
                        p.inc("scalar", "sa", lambda e, pt=pt, qi=qi: e.activation(
                            out=H3B[:, pt * C:(pt + 1) * C], in_=QS[qi][:, :], func=AF.Gelu, bias=cnb3_c(pt)))
                        qrel(qi, "sa")
                    p.wait("tensor", "sa")
                    qwait(2)
                    for kg in range(2):
                        p.inc("tensor", "sp", lambda e, kg=kg: e.matmul(
                            Q2[0:1, :], CN4[:, kg:kg + 1], H3B[:, kg * C:(kg + 1) * C],
                            start=(kg == 0), stop=(kg == 1)))
                    p.wait("scalar", "sp")
                    p.inc("scalar", "sa", lambda e: e.activation(
                        out=TFR[:], in_=Q2[0:1, :], func=AF.Tanh, scale=0.5, bias=RSM[0:1, 1:2]))
                    qrel(2, "sa")
                    p.wait("vector", "sa")
                    p.inc("vector", "sv", lambda e: e.tensor_scalar(
                        out=CPR[:], in0=TFR[:], scalar1=RSM[0:1, 8:9], scalar2=RSM[0:1, 9:10],
                        op0=ALU.mult, op1=ALU.add))
                    p.wait("tensor", "sv")
                    qwait(3)
                    p.inc("tensor", "sp", lambda e: e.matmul(Q3[:, :], ONES_ROW_F, CPR[0:1, :], start=True, stop=True))
                    p.wait("vector", "sp")
                    for k in range(KD):
                        src_t, sgn = TSGN[k]
                        p.raw("vector", lambda e, k=k, src_t=src_t, sgn=sgn, SRC=SRC: e.scalar_tensor_tensor(
                            out=MSK1[:], in0=SRC[:, src_t * C:(src_t + 1) * C], scalar=sgn,
                            in1=Q3[:, :], op0=ALU.mult, op1=ALU.mult))
                        p.inc("vector", "sv", lambda e, k=k, SRC=SRC, DST=DST: e.tensor_tensor(
                            out=DST[:, k * C:(k + 1) * C], in0=SRC[:, k * C:(k + 1) * C],
                            in1=MSK1[:], op=ALU.add))
                    qrel(3, "sv")
                    p.barrier(); qreset()
                for k in range(KD):
                    p.inc("vector", "sv", lambda e, k=k: e.tensor_tensor(
                        out=XF[:, k * C:(k + 1) * C], in0=XF[:, k * C:(k + 1) * C],
                        in1=OB[:, k * C:(k + 1) * C], op=ALU.add))
                p.barrier()
                emit_ln(XNB)
                p.barrier(); qreset()

                if tc == 0:
                    p.inc("vector", "sv", lambda e: e.memset(VTD[:], 0))
                    for pt in range(KD):
                        buf = pt % 2
                        p.wait("sync", "sp")
                        p.dma(WS[:, buf * 4096:buf * 4096 + 1024].rearrange("p (g f) -> p g f", g=KD),
                              wv_s.rearrange("(g p) e -> p g e", p=128)[:, :, pt * 128:(pt + 1) * 128])
                        qi = pt % 2
                        qwait(qi)
                        p.wait("tensor", "sd")
                        p.wait("tensor", "sv")
                        for kg in range(KD):
                            p.inc("tensor", "sp", lambda e, kg=kg, buf=buf, qi=qi: e.matmul(
                                QS[qi][:, 0:14], WS[:, buf * 4096 + kg * 128:buf * 4096 + (kg + 1) * 128],
                                XNB[:, kg * C:kg * C + 14], start=(kg == 0), stop=(kg == KD - 1)))
                        p.wait("scalar", "sp")
                        p.inc("scalar", "sa", lambda e, pt=pt, qi=qi: e.activation(
                            out=VAL[:, pt * 14:(pt + 1) * 14], in_=QS[qi][:, 0:14], func=AF.Identity, bias=bv_c(pt)))
                        qrel(qi, "sa")
                    p.barrier(); qreset()
                    for pt in range(KD):
                        p.wait("tensor", "sv")
                        p.inc("tensor", "sp", lambda e, pt=pt: e.matmul(
                            P2[0:14, 0:128], VAL[:, pt * 14:(pt + 1) * 14], EYE[:, :],
                            start=True, stop=True))
                        p.wait("vector", "sp")
                        rb = 64 * (pt % 2)
                        p.raw("vector", lambda e, pt=pt, rb=rb: e.tensor_copy(
                            VTD[rb:rb + 14, pt * 128:pt * 128 + 64], P2[0:14, 0:64]))
                        p.inc("vector", "sv", lambda e, pt=pt, rb=rb: e.tensor_copy(
                            VTD[rb + 32:rb + 46, pt * 128 + 64:pt * 128 + 128], P2[0:14, 64:128]))
                    p.barrier(); qreset()
                for (wres, dst_fn) in ((WOX, 0), (WOY, 1), (WAT, 2)):
                    qi = dst_fn
                    qwait(qi)
                    for kg in range(KD):
                        p.inc("tensor", "sp", lambda e, kg=kg, wres=wres, qi=qi: e.matmul(
                            QS[qi][:, :], wres[:, kg * 128:(kg + 1) * 128],
                            XNB[:, kg * C:(kg + 1) * C], start=(kg == 0), stop=(kg == KD - 1)))
                p.wait("vector", "sp")
                p.raw("vector", lambda e: e.scalar_tensor_tensor(
                    out=IX[:], in0=Q0[:, :], scalar=boffx_c, in1=REFC[:], op0=ALU.add, op1=ALU.add))
                p.raw("vector", lambda e: e.tensor_scalar(
                    out=IY[:], in0=Q1[:, :], scalar1=boffy_c, scalar2=-0.5, op0=ALU.add, op1=ALU.add))
                p.inc("vector", "sv", lambda e: e.tensor_scalar(
                    out=AWL[:], in0=Q2[:, :], scalar1=batt_c, scalar2=None, op0=ALU.add))
                qrel(0, "sv"); qrel(1, "sv"); qrel(2, "sv")
                p.wait("tensor", "sv")
                qwait(3)
                p.inc("tensor", "sp", lambda e: e.matmul(Q3[:, :], SAVG[:, :], AWL[:], start=True, stop=True))
                p.wait("vector", "sp")
                p.inc("vector", "sv", lambda e: e.tensor_tensor(out=M4[:], in0=AWL[:], in1=Q3[:, :], op=ALU.subtract))
                qrel(3, "sv")
                p.wait("scalar", "sv")
                p.inc("scalar", "sa", lambda e: e.activation(out=EB[:], in_=M4[:], func=AF.Exp))
                p.wait("tensor", "sa")
                qwait(4)
                p.inc("tensor", "sp", lambda e: e.matmul(Q4[:, :], SSUM[:, :], EB[:], start=True, stop=True))
                p.wait("vector", "sp")
                p.raw("vector", lambda e: e.reciprocal(M4[:], Q4[:, :]))
                p.raw("vector", lambda e: e.tensor_tensor(out=AWL[:], in0=EB[:], in1=M4[:], op=ALU.mult))
                qrel(4, "sv")
                p.inc("vector", "sv", lambda e: e.engine_nop() if False else e.tensor_copy(RSM[0:1, 15:16], RSM[0:1, 15:16]))
                p.wait("scalar", "sv")
                p.inc("scalar", "sa", lambda e: e.activation(out=HAT[:], in_=IX[:], func=AF.Abs))
                p.wait("vector", "sa")
                p.raw("vector", lambda e: e.tensor_scalar(out=HAT[:], in0=HAT[:], scalar1=1.0, scalar2=-1.0, op0=ALU.min, op1=ALU.mult))
                p.inc("vector", "sv", lambda e: e.scalar_tensor_tensor(
                    out=GB[:], in0=HAT[:], scalar=1.0, in1=AWL[:], op0=ALU.add, op1=ALU.mult))
                p.barrier(); qreset()
                for j in range(14):
                    p.wait("scalar", "sv")
                    p.wait("scalar", "sp")
                    bias_j = 0.0 if j == 0 else -float(j)
                    p.inc("scalar", "sa", lambda e, bias_j=bias_j: e.activation(
                        out=HAT[:], in_=IY[:], func=AF.Abs, bias=bias_j))
                    p.wait("vector", "sa")
                    p.raw("vector", lambda e: e.scalar_tensor_tensor(
                        out=MSK1[:], in0=HAT[:], scalar=1.0, in1=GB[:], op0=ALU.min, op1=ALU.mult))
                    p.inc("vector", "sv", lambda e: e.tensor_tensor(
                        out=MSK2[:], in0=GB[:], in1=MSK1[:], op=ALU.subtract))
                    p.wait("tensor", "sv")
                    for grp in range(4):
                        p.inc("tensor", "sp", lambda e, j=j, grp=grp: e.matmul(
                            QS[grp][:, :], SELC[:, grp * 141 + 13 - j:grp * 141 + 13 - j + 128],
                            MSK2[:], start=(j == 0), stop=(j == 13)))
                p.wait("vector", "sp")
                for grp in range(4):
                    p.inc("vector", "sv", lambda e, grp=grp: e.tensor_copy(
                        AJ[:, grp * C:(grp + 1) * C], QS[grp][:, :]))
                p.barrier(); qreset()
                for pt in range(KD):
                    qi = pt % 2
                    qwait(qi)
                    rb = 64 * (pt % 2)
                    p.inc("tensor", "sp", lambda e, pt=pt, qi=qi, rb=rb: e.matmul(
                        QS[qi][:, :], VTD[rb:rb + 64, pt * 128:(pt + 1) * 128],
                        AJ[rb:rb + 64, (pt // 2) * C:(pt // 2 + 1) * C],
                        start=True, stop=True))
                    p.wait("scalar", "sp")
                    p.inc("scalar", "sa", lambda e, pt=pt, qi=qi: e.copy(
                        out=ATN[:, pt * C:(pt + 1) * C], in_=QS[qi][:, :]))
                    qrel(qi, "sa")
                p.barrier(); qreset()
                wfree = [("sv", 0), ("sv", 0)]
                wld = [0, 0]
                for pt in range(KD):
                    buf = pt % 2
                    if pt == 0:
                        wld[0] = p.dma(WS[:, 0:1024].rearrange("p (g f) -> p g f", g=KD),
                                       wout_s.rearrange("(g p) e -> p g e", p=128)[:, :, 0:128])
                    if pt + 1 < KD:
                        nb = (pt + 1) % 2
                        p.wait("sync", wfree[nb][0], wfree[nb][1])
                        wld[nb] = p.dma(WS[:, nb * 4096:nb * 4096 + 1024].rearrange("p (g f) -> p g f", g=KD),
                                        wout_s.rearrange("(g p) e -> p g e", p=128)[:, :, (pt + 1) * 128:(pt + 2) * 128])
                    qi = pt % 2
                    qwait(qi)
                    p.wait("tensor", "sd", wld[buf])
                    for kg in range(KD):
                        p.inc("tensor", "sp", lambda e, kg=kg, buf=buf, qi=qi: e.matmul(
                            QS[qi][:, :], WS[:, buf * 4096 + kg * 128:buf * 4096 + (kg + 1) * 128],
                            ATN[:, kg * C:(kg + 1) * C], start=(kg == 0), stop=(kg == KD - 1)))
                    wfree[buf] = ("sp", p.cnt["sp"])
                    p.wait("scalar", "sp")
                    p.inc("scalar", "sa", lambda e, pt=pt, qi=qi: e.activation(
                        out=O1B[:, pt * C:(pt + 1) * C], in_=QS[qi][:, :], func=AF.Identity, bias=bout_c(pt)))
                    qrel(qi, "sa")
                p.barrier(); qreset()
                wfree = [("sv", 0), ("sv", 0)]
                wld = [0, 0]
                for pt in range(KD):
                    buf = pt % 2
                    if pt == 0:
                        wld[0] = p.dma(WS[:, 0:1024].rearrange("p (g f) -> p g f", g=KD),
                                       wproj_s.rearrange("(g p) e -> p g e", p=128)[:, :, 0:128])
                    if pt + 1 < KD:
                        nb = (pt + 1) % 2
                        p.wait("sync", wfree[nb][0], wfree[nb][1])
                        wld[nb] = p.dma(WS[:, nb * 4096:nb * 4096 + 1024].rearrange("p (g f) -> p g f", g=KD),
                                        wproj_s.rearrange("(g p) e -> p g e", p=128)[:, :, (pt + 1) * 128:(pt + 2) * 128])
                    qi = pt % 2
                    qwait(qi)
                    p.wait("tensor", "sd", wld[buf])
                    for kg in range(KD):
                        p.inc("tensor", "sp", lambda e, kg=kg, buf=buf, qi=qi: e.matmul(
                            QS[qi][:, :], WS[:, buf * 4096 + kg * 128:buf * 4096 + (kg + 1) * 128],
                            O1B[:, kg * C:(kg + 1) * C], start=(kg == 0), stop=(kg == KD - 1)))
                    wfree[buf] = ("sp", p.cnt["sp"])
                    p.wait("vector", "sp")
                    p.inc("vector", "sv", lambda e, pt=pt, qi=qi: e.scalar_tensor_tensor(
                        out=XF[:, pt * C:(pt + 1) * C], in0=QS[qi][:, :], scalar=bproj_c(pt),
                        in1=XF[:, pt * C:(pt + 1) * C], op0=ALU.add, op1=ALU.add))
                    qrel(qi, "sv")
                p.barrier(); qreset()
                emit_ln(XNB)
                p.barrier(); qreset()
                wslots = [WS[:, i * 1024:(i + 1) * 1024] for i in range(8)]
                sfree = [("sv", 0)] * 8
                sld = [0] * 8
                for pre in range(4):
                    sld[pre] = p.dma(wslots[pre].rearrange("p (g f) -> p g f", g=KD),
                                     wq1_s.rearrange("(g p) e -> p g e", p=128)[:, :, pre * 128:(pre + 1) * 128])
                for et in range(32):
                    sl = et % 8
                    if et + 4 < 32:
                        nsl = (et + 4) % 8
                        p.wait("sync", sfree[nsl][0], sfree[nsl][1])
                        sld[nsl] = p.dma(wslots[nsl].rearrange("p (g f) -> p g f", g=KD),
                                         wq1_s.rearrange("(g p) e -> p g e", p=128)[:, :, (et + 4) * 128:(et + 5) * 128])
                    qi = et % 4
                    qwait(qi)
                    p.wait("tensor", "sd", sld[sl])
                    for kg in range(KD):
                        p.inc("tensor", "sp", lambda e, kg=kg, sl=sl, qi=qi: e.matmul(
                            QS[qi][:, :], wslots[sl][:, kg * 128:(kg + 1) * 128],
                            XNB[:, kg * C:(kg + 1) * C], start=(kg == 0), stop=(kg == KD - 1)))
                    sfree[sl] = ("sp", p.cnt["sp"])
                    p.wait("scalar", "sp")
                    p.inc("scalar", "sa", lambda e, et=et, qi=qi: e.activation(
                        out=HB[:, et * C:(et + 1) * C], in_=QS[qi][:, :], func=AF.Relu, bias=b1_c(et)))
                    qrel(qi, "sa")
                p.barrier(); qreset()
                wfree = [("sv", 0), ("sv", 0)]
                wld = [0, 0]
                for pt in range(KD):
                    buf = pt % 2
                    if pt == 0:
                        wld[0] = p.dma(WS[:, 0:4096].rearrange("p (g f) -> p g f", g=32),
                                       wq2_s.rearrange("(g p) e -> p g e", p=128)[:, :, 0:128])
                    if pt + 1 < KD:
                        nb = (pt + 1) % 2
                        p.wait("sync", wfree[nb][0], wfree[nb][1])
                        wld[nb] = p.dma(WS[:, nb * 4096:(nb + 1) * 4096].rearrange("p (g f) -> p g f", g=32),
                                        wq2_s.rearrange("(g p) e -> p g e", p=128)[:, :, (pt + 1) * 128:(pt + 2) * 128])
                    qi = pt % 2
                    qwait(qi)
                    p.wait("tensor", "sd", wld[buf])
                    for kg in range(32):
                        p.inc("tensor", "sp", lambda e, kg=kg, buf=buf, qi=qi: e.matmul(
                            QS[qi][:, :], WS[:, buf * 4096 + kg * 128:buf * 4096 + (kg + 1) * 128],
                            HB[:, kg * C:(kg + 1) * C], start=(kg == 0), stop=(kg == 31)))
                    wfree[buf] = ("sp", p.cnt["sp"])
                    p.wait("vector", "sp")
                    p.inc("vector", "sv", lambda e, pt=pt, qi=qi: e.scalar_tensor_tensor(
                        out=XF[:, pt * C:(pt + 1) * C], in0=QS[qi][:, :], scalar=b2_c(pt),
                        in1=XF[:, pt * C:(pt + 1) * C], op0=ALU.add, op1=ALU.add))
                    qrel(qi, "sv")
                    p.wait("sync", "sv")
                    p.dma(yT[b][pt * 128:(pt + 1) * 128, t0:t0 + C], XF[:, pt * C:(pt + 1) * C])
                p.barrier(); qreset()
        p.wait("sync", "sd")
        p.build_block()
        es.close()
    return nc


def run_fused(inputs):
    nc = build_fused()
    maps_a = prep_phase_a_inputs(inputs)
    maps_b = prep_phase_b_inputs(inputs, np.zeros((16, 2, 2), np.float32))
    maps = []
    for c_ in range(8):
        m = dict(maps_a[c_])
        mb = dict(maps_b[c_])
        mb.pop("x1T", None)
        mb.pop("consts_b", None)
        mb.pop("consts_f", None)
        m.update(mb)
        maps.append(m)
    res = run_bass_kernel_spmd(nc, maps, core_ids=list(range(8)))
    y = np.empty((16, T, D), np.float32)
    for c_ in range(8):
        y[c_ * NB:(c_ + 1) * NB] = res.results[c_]["yT"].transpose(0, 2, 1)
    return y
